# version 06
# baseline (speedup 1.0000x reference)
import numpy as np
import ml_dtypes
from contextlib import ExitStack
import concourse.bass as bass
import concourse.mybir as mybir
from concourse.bass_utils import run_bass_kernel_spmd

F32 = mybir.dt.float32
BF16 = mybir.dt.bfloat16
ALU = mybir.AluOpType
ACT = mybir.ActivationFunctionType
AXX = mybir.AxisListType.X

BL = 2
S = 2048
D = 1024
H = 16
NT = 16
NJ = 8
QCW = 512
NQC = 4
NSL = 64
DQ = 256
DH = 512
SCALE = 0.125
LN_EPS = 1e-5

SHUF_MASK = [(i + 1 if i % 2 == 0 else i - 1) for i in range(32)]


def _rope_tables():
    HD = 64
    inv = 1.0 / (10000.0 ** (np.arange(0, HD, 2, dtype=np.float64) / HD))
    pos = np.arange(S, dtype=np.float64)
    fr = np.outer(pos, inv)
    emb = np.concatenate([fr, fr], -1)
    cos = np.cos(emb).T.astype(np.float32)
    sin = np.sin(emb).T.astype(np.float32)
    sgn = np.where(np.arange(HD) % 2 == 0, -1.0, 1.0).astype(np.float32)
    sin = sin * sgn[:, None]
    cosb = np.concatenate([cos, cos], 0)
    sinb = np.concatenate([sin, sin], 0)
    return np.ascontiguousarray(cosb), np.ascontiguousarray(sinb)


def _mext_table():
    kk = np.arange(128)[:, None]
    c = np.arange(896)[None, :]
    return np.ascontiguousarray((c >= kk + 384).astype(np.float32))


class P:

    def __init__(self, nc):
        self.nc = nc
        self.q = {k: [] for k in ("T", "V", "S", "G", "Y")}
        self.c = dict(T=0, V=0, S=0, G=0, DY=0, DG=0)

    def _emit(self, qn, fn, inc, waits):
        w = tuple(waits)
        if inc is not None:
            self.c[inc[0]] += inc[1]
        def run(e, sems, fn=fn, inc=inc, w=w):
            for snm, v in w:
                if v > 0:
                    e.wait_ge(sems[snm], v)
            ins = fn(e)
            if inc is not None and ins is not None:
                ins.then_inc(sems[inc[0]], inc[1])
        self.q[qn].append(run)

    def T(self, fn, inc=False, waits=()):
        self._emit("T", fn, ("T", 1) if inc else None, waits)

    def V(self, fn, inc=False, waits=()):
        self._emit("V", fn, ("V", 1) if inc else None, waits)

    def Sc(self, fn, inc=False, waits=()):
        self._emit("S", fn, ("S", 1) if inc else None, waits)

    def G(self, fn, inc=False, waits=()):
        self._emit("G", fn, ("G", 1) if inc else None, waits)

    def Ydma(self, out, in_, waits=()):
        self._emit("Y", lambda e, o=out, i=in_: e.dma_start(out=o, in_=i),
                   ("DY", 16), waits)
        return self.c["DY"]

    def Gdma(self, out, in_, waits=()):
        self._emit("G", lambda e, o=out, i=in_: e.dma_start(out=o, in_=i),
                   ("DG", 16), waits)
        return self.c["DG"]

    def Ywait(self, waits):
        self._emit("Y", lambda e: None, None, waits)


def build_nc(dbg=False):
    nc = bass.Bass("TRN2", target_bir_lowering=False)
    dt = {}

    def din(name, shape):
        dt[name] = nc.dram_tensor(name, list(shape), F32, kind="ExternalInput").ap()

    din("x", [BL * S, D])
    for nm, shp in [("WqT", [D, D]), ("WkT", [D, D]), ("WvT", [D, D]), ("WoT", [D, D]),
                    ("w1", [2 * D, DH]), ("w2", [DH, NSL]),
                    ("WihT", [D, 3 * D]), ("WhhT", [D, 3 * D]),
                    ("wq", [D, DQ]), ("wk", [D, DQ]), ("wv", [D, D]),
                    ("wvp", [D, D]), ("wo", [D, D]), ("slot_init", [NSL, D]),
                    ("b1r", [1, DH]), ("b2bc", [128, NSL]), ("taur", [128, 1]),
                    ("bihc", [64, 3 * D]), ("bhhc", [64, 3 * D]),
                    ("bqc", [64, DQ]), ("bkc", [64, DQ]), ("bvc", [64, D]),
                    ("lngc", [64, D]), ("lnbc", [64, D]), ("bvpc", [64, D]), ("boc", [128, D]),
                    ("identc", [128, 128]), ("cosc", [128, S]), ("sinc", [128, S]),
                    ("mextc", [128, 896])]:
        din(nm, shp)
    out_d = nc.dram_tensor("out", [BL * S, D], F32, kind="ExternalOutput").ap()
    y_d = nc.dram_tensor("y_scr", [BL * S, D], F32, kind="Internal").ap()
    gh_d = nc.dram_tensor("gh_scr", [NSL, 3 * D], F32, kind="Internal").ap()
    taps = {}
    if dbg:
        for nm, shp in [("t_qT", [128, NJ * S]), ("t_kT", [128, NJ * S]),
                        ("t_vp", [128, NT * H * 65]), ("t_attn", [128, NJ * QCW]),
                        ("t_yn", [128, NT * D]), ("t_hT", [128, 4 * S]),
                        ("t_alpha", [128, NT * NSL]), ("t_si", [NSL, D]),
                        ("t_snew", [NSL, D]), ("t_y2n", [NSL, D]), ("t_sv", [NSL, D]),
                        ("t_gh", [NSL, 3 * D]), ("t_xnT", [128, NJ * S]),
                        ("t_sx", [128, BL * NT]), ("t_dbgst", [128, 64])]:
            taps[nm] = nc.dram_tensor(nm, shp, F32, kind="ExternalOutput").ap()

    p = P(nc)
    ctx = ExitStack()
    E = ctx.enter_context

    sems = {}
    for snm in ("T", "V", "S", "G", "DY", "DG"):
        sems[snm] = E(nc.semaphore("sem" + snm))

    psS = E(nc.psum_tensor("psS", [128, 2048], F32))
    pv = E(nc.psum_tensor("pv", [128, 1024], F32))
    pv2 = E(nc.psum_tensor("pv2", [128, 1024], F32))

    identb = E(nc.sbuf_tensor("identb", [128, 128], BF16))
    onescol = E(nc.sbuf_tensor("onescol", [128, 1], BF16))
    onesrow = E(nc.sbuf_tensor("onesrow", [1, 64], BF16))
    sxbuf = E(nc.sbuf_tensor("sxbuf", [128, BL * NT], F32))
    cols = E(nc.sbuf_tensor("cols", [128, 16], F32))
    rowstage = E(nc.sbuf_tensor("rowstage", [1, D], BF16))
    rowf = E(nc.sbuf_tensor("rowf", [1, DH], F32))
    si0b = E(nc.sbuf_tensor("si0b", [NSL, D], BF16))
    si0T = E(nc.sbuf_tensor("si0T", [128, NJ * NSL], BF16))
    rt_col = E(nc.sbuf_tensor("rt_col", [128, 1], F32))
    epscol = E(nc.sbuf_tensor("epscol", [128, 1], F32))
    rt_row = E(nc.sbuf_tensor("rt_row", [1, 4], F32))
    b1c1_col = E(nc.sbuf_tensor("b1c1_col", [128, 4], F32))
    dbgst = E(nc.sbuf_tensor("dbgst", [128, 64], F32))
    bnst = E(nc.sbuf_tensor("bnst", [128, 12], F32))
    bnagg = E(nc.sbuf_tensor("bnagg", [128, 2], F32))

    p.Gdma(identb[:], dt["identc"][:])
    p.Gdma(si0b[:], dt["slot_init"][:])
    dg_si = p.c["DG"]
    p.V(lambda e: e.memset(onescol[:], 1.0))
    p.V(lambda e: e.memset(onesrow[:], 1.0))
    p.V(lambda e: e.memset(epscol[:], LN_EPS), inc=True)

    p.Gdma(rt_col[:], dt["taur"][:])
    dg_tau = p.c["DG"]
    p.Sc(lambda e: e.activation(out=rt_col[:], in_=rt_col[:], func=ACT.Abs),
         inc=True, waits=[("DG", dg_tau)])
    p.V(lambda e: e.tensor_scalar_add(rt_col[:], rt_col[:], 0.1),
        waits=[("S", p.c["S"])])
    p.V(lambda e: e.reciprocal(rt_col[:], rt_col[:]), inc=True)
    p.V(lambda e: e.tensor_copy(out=rt_row[0:1, 3:4], in_=rt_col[0:1, 0:1]), inc=True)
    v_rt = p.c["V"]

    for jc in range(NJ):
        p.T(lambda e, jc=jc: e.matmul(out=pv2[:, jc * 64:(jc + 1) * 64], lhsT=si0b[0:NSL, jc * 128:(jc + 1) * 128], rhs=identb[0:NSL, 0:NSL], start=True, stop=True),
            inc=(jc == NJ - 1), waits=[("DG", dg_si)] if jc == 0 else ())
    p.V(lambda e: e.tensor_copy(out=si0T[:], in_=pv2[:, 0:NJ * 64]), inc=True,
        waits=[("T", p.c["T"])])
    v_si0T = p.c["V"]

    sctx = ExitStack()
    whh_sb = sctx.enter_context(nc.sbuf_tensor("WhhT_sb", [128, NJ * 3 * D], BF16))
    bhh_bc = sctx.enter_context(nc.sbuf_tensor("bhh_bc", [64, 3 * D], BF16))
    gh_stage = sctx.enter_context(nc.sbuf_tensor("gh_stage", [64, 3 * D], F32))
    if True:
        p.Gdma(whh_sb[:], dt["WhhT"][:].rearrange("(c p) f -> p c f", p=128))
        dg_whh = p.c["DG"]
        p.Gdma(bhh_bc[:], dt["bhhc"][:])
        dg_bhh = p.c["DG"]
        evac_v = {0: 0, 1: 0, 2: 0, 3: 0}
        for rc in range(6):
            qtr = rc % 4
            w_ = [("V", evac_v[qtr])]
            if rc == 0:
                w_ += [("DG", dg_whh), ("V", v_si0T)]
            for dc in range(NJ):
                p.T(lambda e, rc=rc, dc=dc, q_=qtr: e.matmul(
                        out=psS[0:NSL, q_ * 512:(q_ + 1) * 512],
                        lhsT=si0T[:, dc * NSL:(dc + 1) * NSL],
                        rhs=whh_sb[:, dc * 3 * D + rc * 512: dc * 3 * D + (rc + 1) * 512],
                        start=(dc == 0), stop=(dc == NJ - 1)),
                    inc=(dc == NJ - 1), waits=w_ if dc == 0 else ())
            tmm = p.c["T"]
            p.V(lambda e, rc=rc, q_=qtr: e.tensor_add(
                    out=gh_stage[:, rc * 512:(rc + 1) * 512],
                    in0=psS[0:NSL, q_ * 512:(q_ + 1) * 512],
                    in1=bhh_bc[:, rc * 512:(rc + 1) * 512]),
                inc=True, waits=[("T", tmm), ("DG", dg_bhh)])
            evac_v[qtr] = p.c["V"]
        p.Gdma(gh_d[:], gh_stage[:], waits=[("V", p.c["V"])])
        dg_gh_store = p.c["DG"]
        if dbg:
            p.Gdma(taps["t_gh"][:], gh_stage[:], waits=[("V", p.c["V"])])
    sctx.close()
    t_after_setup = p.c["T"]

    actx = ExitStack()
    AE = actx.enter_context
    cosb = AE(nc.sbuf_tensor("cosb", [128, S], BF16))
    sinb = AE(nc.sbuf_tensor("sinb", [128, S], BF16))
    mextb = AE(nc.sbuf_tensor("mextb", [128, 896], BF16))
    xstage = AE(nc.sbuf_tensor("xstage", [128, 2 * D], F32))
    xnt = AE(nc.sbuf_tensor("xnt", [128, D], BF16))
    xnT = AE(nc.sbuf_tensor("xnT", [128, NJ * S], BF16))
    wslA = AE(nc.sbuf_tensor("wslotA", [128, 4096], BF16))
    wslB = AE(nc.sbuf_tensor("wslotB", [128, 4096], BF16))
    qT = AE(nc.sbuf_tensor("qT", [128, NJ * S], BF16))
    kT = AE(nc.sbuf_tensor("kT", [128, NJ * S], BF16))
    vpad = AE(nc.sbuf_tensor("vpad", [128, NT * H * 65], BF16))
    rtmp = AE(nc.sbuf_tensor("rtmp", [128, D], BF16))
    Psb = AE(nc.sbuf_tensor("Psb", [128, 3 * 1024], BF16))
    rl_sb = AE(nc.sbuf_tensor("rl_sb", [1, 2048], BF16))
    rl_bc = AE(nc.sbuf_tensor("rl_bc", [64, 2048], BF16))
    lnrow = AE(nc.sbuf_tensor("lnrow", [1, 1024], F32))
    attn_sb = AE(nc.sbuf_tensor("attn_sb", [128, NJ * QCW], BF16))
    o2_sb = AE(nc.sbuf_tensor("o2_sb", [128, D], F32))
    if True:

        p.Gdma(cosb[:], dt["cosc"][:], waits=[("T", t_after_setup)])
        p.Gdma(sinb[:], dt["sinc"][:])
        p.Gdma(mextb[:], dt["mextc"][:])
        dg_tabs = p.c["DG"]
        p.V(lambda e: e.memset(
            vpad[:].rearrange("p (th w) -> p th w", w=65)[:, :, 64:65], 1.0),
            inc=True)
        v_ones = p.c["V"]

        xstage_cons = [0, 0]
        for b in range(BL):
            tw = t_after_setup if b == 0 else p.c["T"]
            dgWq = [p.Gdma(wslA[:], dt["WqT"][:, 0:512].rearrange("(c p) f -> p c f", p=128),
                           waits=[("T", tw)]),
                    p.Gdma(wslB[:], dt["WqT"][:, 512:1024].rearrange("(c p) f -> p c f", p=128))]
            tr_done = p.c["T"]
            for tt in range(NT):
                sl = tt % 2
                xw = [("V", xstage_cons[sl])]
                if b == 0 and tt < 2:
                    xw += [("DG", dg_gh_store), ("T", t_after_setup)]
                dy = p.Ydma(xstage[:, sl * D:(sl + 1) * D],
                            x_d_slice(dt, b, tt), waits=xw)
                xs = xstage[:, sl * D:(sl + 1) * D]
                idx = b * NT + tt
                p.V(lambda e, xs=xs: e.bn_stats(out=bnst[:, 0:6], in_=xs[:, 0:512]),
                    waits=[("DY", dy)])
                p.V(lambda e, xs=xs: e.bn_stats(out=bnst[:, 6:12], in_=xs[:, 512:1024]),
                    inc=True)
                vbs = p.c["V"]
                p.V(lambda e: e.bn_aggr(out=bnagg[:], in_=bnst[:]), inc=True,
                    waits=[("V", vbs)])
                vv = p.c["V"]
                p.V(lambda e, idx=idx: e.tensor_copy(out=sxbuf[:, idx:idx + 1],
                                                     in_=bnagg[:, 0:1]),
                    waits=[("V", vv)])
                p.Sc(lambda e: e.activation(out=cols[:, 4:5], in_=bnagg[:, 1:2],
                                            func=ACT.Ln, bias=epscol[:, 0:1], scale=1.0),
                     inc=True, waits=[("V", vv)])
                sl1 = p.c["S"]
                p.Sc(lambda e: e.activation(out=cols[:, 5:6], in_=cols[:, 4:5],
                                            func=ACT.Exp, scale=-0.5), inc=True,
                     waits=[("S", sl1)])
                ss = p.c["S"]
                if dbg and b == 0:
                    p.V(lambda e, tt=tt: e.tensor_copy(out=dbgst[:, tt * 4:tt * 4 + 4],
                                                       in_=cols[:, 0:4]),
                        waits=[("S", ss)])
                    p.V(lambda e, tt=tt: e.tensor_copy(out=dbgst[:, 63:64],
                                                       in_=cols[:, 5:6]))
                p.V(lambda e, xs=xs: e.tensor_scalar(
                        out=xnt[:], in0=xs, scalar1=bnagg[:, 0:1], scalar2=cols[:, 5:6],
                        op0=ALU.subtract, op1=ALU.mult),
                    inc=True, waits=[("S", ss), ("T", tr_done)])
                xstage_cons[sl] = p.c["V"]
                vxn = p.c["V"]
                for jc in range(NJ):
                    p.T(lambda e, jc=jc: e.matmul(out=pv2[:, jc * 128:(jc + 1) * 128], lhsT=xnt[:, jc * 128:(jc + 1) * 128], rhs=identb[:], start=True, stop=True),
                        inc=(jc == NJ - 1),
                        waits=[("V", vxn)] if jc == 0 else ())
                tposted = p.c["T"]
                p.V(lambda e, tt=tt: e.tensor_copy(
                        out=xnT[:].rearrange("p (c w) -> p c w", w=S)[:, :, tt * 128:(tt + 1) * 128],
                        in_=pv2[:, 0:1024]), inc=True, waits=[("T", tposted)])
                tr_done = p.c["T"]
                if tt == 7:
                    v_xnT_h0 = p.c["V"]
            v_xnT = p.c["V"]
            v_xnT_h = [v_xnT_h0, v_xnT]

            s0 = p.c["S"]
            evac_s = {0: s0, 1: s0, ("wo", 0): s0, ("wo", 1): s0}

            def proj_fm(dst, wdg, rope):
                for ic in range(NJ):
                    sl = wslA if ic < 4 else wslB
                    wcol = (ic % 4) * 128
                    for th in range(2):
                        half = (ic * 2 + th) % 2
                        w_ = [("DG", wdg[0 if ic < 4 else 1]), ("V", v_xnT_h[th]),
                              ("S", evac_s[half])]
                        for sub in range(2):
                            for j in range(NJ):
                                p.T(lambda e, j=j, sl=sl, wcol=wcol, th=th, sub=sub, half=half:
                                    e.matmul(
                                        out=psS[:, half * 1024 + sub * 512:
                                                half * 1024 + (sub + 1) * 512],
                                        lhsT=sl[:, j * 512 + wcol: j * 512 + wcol + 128],
                                        rhs=xnT[:, j * S + th * 1024 + sub * 512:
                                                j * S + th * 1024 + (sub + 1) * 512],
                                        start=(j == 0), stop=(j == NJ - 1)),
                                    inc=(sub == 1 and j == NJ - 1),
                                    waits=w_ if (sub == 0 and j == 0) else ())
                        tmm = p.c["T"]
                        dslice = dst[:, ic * S + th * 1024: ic * S + (th + 1) * 1024]
                        p.Sc(lambda e, dslice=dslice, half=half: e.copy(
                                out=dslice, in_=psS[:, half * 1024:(half + 1) * 1024]),
                             inc=True, waits=[("T", tmm)])
                        evac_s[half] = p.c["S"]
                        if rope:
                            scp = p.c["S"]
                            cs = cosb[:, th * 1024:(th + 1) * 1024]
                            sn = sinb[:, th * 1024:(th + 1) * 1024]
                            p.V(lambda e, dslice=dslice: e.stream_shuffle(
                                    out=rtmp[:], in_=dslice, mask=SHUF_MASK),
                                waits=[("S", scp), ("DG", dg_tabs)])
                            p.V(lambda e, sn=sn: e.tensor_mul(out=rtmp[:], in0=rtmp[:], in1=sn))
                            p.V(lambda e, dslice=dslice, cs=cs: e.tensor_mul(
                                    out=dslice, in0=dslice, in1=cs))
                            p.V(lambda e, dslice=dslice: e.tensor_add(
                                    out=dslice, in0=dslice, in1=rtmp[:]), inc=True)

            if dbg and b == 0:
                p.Gdma(taps["t_xnT"][:], xnT[:], waits=[("V", v_xnT)])
                p.Gdma(taps["t_sx"][:], sxbuf[:], waits=[("V", v_xnT)])
                p.Gdma(taps["t_dbgst"][:], dbgst[:], waits=[("V", v_xnT)])
            proj_fm(qT, dgWq, rope=True)
            t_q = p.c["T"]
            dgWk = [p.Gdma(wslA[:], dt["WkT"][:, 0:512].rearrange("(c p) f -> p c f", p=128),
                           waits=[("T", t_q)]),
                    p.Gdma(wslB[:], dt["WkT"][:, 512:1024].rearrange("(c p) f -> p c f", p=128))]
            proj_fm(kT, dgWk, rope=True)
            t_k = p.c["T"]
            dgWv = [p.Gdma(wslA[:], dt["WvT"][:, 0:512].rearrange("(c p) f -> p c f", p=128),
                           waits=[("T", t_k)]),
                    p.Gdma(wslB[:], dt["WvT"][:, 512:1024].rearrange("(c p) f -> p c f", p=128))]
            for tt in range(NT):
                half = tt % 2
                w_ = [("DG", dgWv[1]), ("V", v_ones), ("S", evac_s[half])]
                for hf in range(2):
                    sl = wslA if hf == 0 else wslB
                    for j in range(NJ):
                        p.T(lambda e, tt=tt, hf=hf, j=j, sl=sl, half=half: e.matmul(
                                out=psS[:, half * 1024 + hf * 512: half * 1024 + hf * 512 + 512],
                                lhsT=xnT[:, j * S + tt * 128: j * S + tt * 128 + 128],
                                rhs=sl[:, j * 512:(j + 1) * 512],
                                start=(j == 0), stop=(j == NJ - 1)),
                            inc=(hf == 1 and j == NJ - 1),
                            waits=w_ if (hf == 0 and j == 0) else ())
                tmm = p.c["T"]
                p.Sc(lambda e, tt=tt, half=half: e.copy(
                        out=vpad[:, tt * H * 65:(tt + 1) * H * 65]
                            .rearrange("p (h w) -> p h w", w=65)[:, :, 0:64],
                        in_=psS[:, half * 1024:(half + 1) * 1024]
                            .rearrange("p (h w) -> p h w", w=64)),
                     inc=True, waits=[("T", tmm)])
                evac_s[half] = p.c["S"]
            t_v = p.c["T"]
            s_v = p.c["S"]
            dgWo = [p.Gdma(wslA[:], dt["WoT"][:, 0:512].rearrange("(c p) f -> p c f", p=128),
                           waits=[("T", t_v)]),
                    p.Gdma(wslB[:], dt["WoT"][:, 512:1024].rearrange("(c p) f -> p c f", p=128))]

            pv_free = {0: p.c["V"], 1: p.c["V"]}
            attn_free_t = p.c["T"]
            for qc in range(NQC):
                nkt = 4 * (qc + 1)
                def emit_tail(ctx):
                    pvc, rlslot, t_pv, hp = ctx
                    rlr = rl_sb[0:1, (hp % 2) * 1024:(hp % 2) * 1024 + 1024]
                    p.Sc(lambda e, pvc=pvc: e.activation(
                            out=lnrow[0:1, :], in_=pvc[64:65, 0:1024], func=ACT.Ln),
                         inc=True, waits=[("T", t_pv)])
                    s_ln = p.c["S"]
                    p.Sc(lambda e, pvc=pvc, rlr=rlr: e.activation(out=rlr, in_=lnrow[0:1, :],
                                                func=ACT.Exp, scale=-1.0),
                         inc=True, waits=[("S", s_ln)])
                    s_rl = p.c["S"]
                    for hb in range(2):
                        p.T(lambda e, hb=hb, pvc=pvc, rlr=rlr: e.matmul(
                                out=pvc[64:128, hb * 512:(hb + 1) * 512],
                                lhsT=onesrow[0:1, :],
                                rhs=rlr[0:1, hb * 512:(hb + 1) * 512],
                                start=True, stop=True),
                            inc=(hb == 1), waits=[("S", s_rl)] if hb == 0 else ())
                    t_bc = p.c["T"]
                    p.V(lambda e, pvc=pvc, rlslot=rlslot: e.tensor_copy(
                            out=rlslot, in_=pvc[64:128, 0:1024]),
                        inc=True, waits=[("T", t_bc)])
                    v_rlbc = p.c["V"]
                    p.V(lambda e, hp=hp, pvc=pvc, rlslot=rlslot: e.tensor_mul(
                            out=attn_sb[0:64, hp * QCW:(hp + 1) * QCW],
                            in0=pvc[0:64, 0:512], in1=rlslot[:, 0:512]),
                        waits=[("V", v_rlbc), ("T", attn_free_t)])
                    p.V(lambda e, hp=hp, pvc=pvc, rlslot=rlslot: e.tensor_mul(
                            out=attn_sb[64:128, hp * QCW:(hp + 1) * QCW],
                            in0=pvc[0:64, 512:1024], in1=rlslot[:, 512:1024]),
                        inc=True)
                    pv_free[hp % 2] = p.c["V"]

                prev_ctx = None
                for hp in range(NJ):
                    pvc = pv if hp % 2 == 0 else pv2
                    rlslot = rl_bc[:, (hp % 2) * 1024:(hp % 2) * 1024 + 1024]
                    gm = {}
                    exp_done = {}

                    def emit_pv(kt):
                        pslot = Psb[:, (kt % 3) * 1024:(kt % 3) * 1024 + 1024]
                        for hh in range(2):
                            w2_ = [("S", exp_done[kt])] + gm[kt]
                            if kt == 0 and hh == 0:
                                w2_.append(("V", pv_free[hp % 2]))
                            p.T(lambda e, hp=hp, kt=kt, hh=hh, pslot=pslot, nkt=nkt,
                                        pvc=pvc: e.matmul(
                                    out=pvc[0:65, hh * 512:(hh + 1) * 512],
                                    lhsT=vpad[:, kt * H * 65 + (2 * hp + hh) * 65:
                                              kt * H * 65 + (2 * hp + hh) * 65 + 65],
                                    rhs=pslot[:, hh * 512:(hh + 1) * 512],
                                    start=(kt == 0), stop=(kt == nkt - 1)),
                                inc=(hh == 1), waits=w2_ if hh == 0 else ())

                    for kt in range(nkt):
                        par = kt % 2
                        w_ = [("S", evac_s[par])]
                        if kt == 0:
                            w_.append(("V", pv_free[hp % 2]))
                        for hh in range(2):
                            p.T(lambda e, hp=hp, qc=qc, kt=kt, hh=hh, par=par: e.matmul(
                                    out=psS[:, par * 1024 + hh * 512: par * 1024 + hh * 512 + 512],
                                    lhsT=kT[hh * 64:(hh + 1) * 64,
                                            hp * S + kt * 128: hp * S + (kt + 1) * 128],
                                    rhs=qT[hh * 64:(hh + 1) * 64,
                                           hp * S + qc * QCW: hp * S + (qc + 1) * QCW],
                                    start=True, stop=True,
                                    tile_position=(hh * 64, 0)),
                                inc=(hh == 1), waits=w_ if hh == 0 else ())
                        tsc = p.c["T"]
                        pslot = Psb[:, (kt % 3) * 1024:(kt % 3) * 1024 + 1024]
                        p.Sc(lambda e, pslot=pslot, par=par: e.activation(
                                out=pslot, in_=psS[:, par * 1024:(par + 1) * 1024],
                                func=ACT.Exp, scale=SCALE),
                             inc=True, waits=[("T", tsc)])
                        evac_s[par] = p.c["S"]
                        exp_done[kt] = p.c["S"]
                        jdiag = kt - 4 * qc
                        if 0 <= jdiag <= 3:
                            st = 384 - 128 * jdiag
                            mvA = mextb[:, st:st + 512]
                            p.V(lambda e, pslot=pslot, mvA=mvA: e.tensor_mul(
                                    out=pslot[:, 0:512], in0=pslot[:, 0:512], in1=mvA),
                                inc=True, waits=[("S", exp_done[kt])])
                            p.G(lambda e, pslot=pslot, mvA=mvA: e.tensor_mul(
                                    out=pslot[:, 512:1024], in0=pslot[:, 512:1024], in1=mvA),
                                inc=True, waits=[("S", exp_done[kt])])
                            gm[kt] = [("V", p.c["V"]), ("G", p.c["G"])]
                        else:
                            gm[kt] = []
                        if kt > 1:
                            emit_pv(kt - 2)
                    if nkt > 1:
                        emit_pv(nkt - 2)
                    emit_pv(nkt - 1)
                    t_pv = p.c["T"]
                    if prev_ctx is not None:
                        emit_tail(prev_ctx)
                    prev_ctx = (pvc, rlslot, t_pv, hp)
                emit_tail(prev_ctx)
                prev_ctx = None
                if dbg and qc == 0 and b == 0:
                    p.Gdma(taps["t_attn"][:], attn_sb[:], waits=[("V", p.c["V"])])
                v_attn = p.c["V"]
                for ts in range(4):
                    half = ts % 2
                    dyx = p.Ydma(xstage[:, half * D:(half + 1) * D],
                                 x_d_slice(dt, b, qc * 4 + ts),
                                 waits=[("V", xstage_cons[half])])
                    w_ = [("V", v_attn), ("DG", dgWo[1]),
                          ("V", p.c["V"]), ("S", evac_s[half])]
                    for hf in range(2):
                        sl = wslA if hf == 0 else wslB
                        for dc in range(NJ):
                            p.T(lambda e, ts=ts, hf=hf, dc=dc, sl=sl, half=half: e.matmul(
                                    out=psS[:, half * 1024 + hf * 512:
                                            half * 1024 + hf * 512 + 512],
                                    lhsT=attn_sb[:, dc * QCW + ts * 128:
                                                 dc * QCW + ts * 128 + 128],
                                    rhs=sl[:, dc * 512:(dc + 1) * 512],
                                    start=(dc == 0), stop=(dc == NJ - 1)),
                                inc=(hf == 1 and dc == NJ - 1),
                                waits=w_ if (hf == 0 and dc == 0) else ())
                    t_wo = p.c["T"]
                    p.Sc(lambda e, half=half: e.activation(
                            out=o2_sb[:], in_=psS[:, half * 1024:(half + 1) * 1024],
                            func=ACT.Copy),
                         inc=True, waits=[("T", t_wo)])
                    s_ev = p.c["S"]
                    evac_s[half] = s_ev
                    p.V(lambda e: e.bn_stats(out=bnst[:, 0:6], in_=o2_sb[:, 0:512]),
                        waits=[("S", s_ev)])
                    p.V(lambda e: e.bn_stats(out=bnst[:, 6:12], in_=o2_sb[:, 512:1024]),
                        inc=True)
                    vbs = p.c["V"]
                    p.V(lambda e: e.bn_aggr(out=bnagg[:], in_=bnst[:]), inc=True,
                        waits=[("V", vbs)])
                    vst = p.c["V"]
                    p.Sc(lambda e: e.activation(out=cols[:, 4:5], in_=bnagg[:, 1:2],
                                                func=ACT.Ln, bias=epscol[:, 0:1], scale=1.0),
                         inc=True, waits=[("V", vst)])
                    sl1 = p.c["S"]
                    p.Sc(lambda e: e.activation(out=cols[:, 5:6], in_=cols[:, 4:5],
                                                func=ACT.Exp, scale=-0.5), inc=True,
                         waits=[("S", sl1)])
                    s_rs = p.c["S"]
                    p.V(lambda e: e.tensor_scalar(
                            out=o2_sb[:], in0=o2_sb[:], scalar1=bnagg[:, 0:1],
                            scalar2=cols[:, 5:6], op0=ALU.subtract, op1=ALU.mult),
                        waits=[("S", s_rs), ("DY", dyx)])
                    p.V(lambda e, half=half: e.tensor_add(
                            out=xstage[:, half * D:(half + 1) * D], in0=o2_sb[:],
                            in1=xstage[:, half * D:(half + 1) * D]), inc=True)
                    vy = p.c["V"]
                    p.Ydma(y_d[b * S + qc * QCW + ts * 128:
                               b * S + qc * QCW + ts * 128 + 128, :],
                           xstage[:, half * D:(half + 1) * D], waits=[("V", vy)])
                    xstage_cons[half] = vy
                attn_free_t = p.c["T"]
            if dbg and b == 0:
                p.Gdma(taps["t_qT"][:], qT[:], waits=[("V", p.c["V"])])
                p.Gdma(taps["t_kT"][:], kT[:], waits=[("V", p.c["V"])])
                p.Gdma(taps["t_vp"][:], vpad[:], waits=[("S", p.c["S"])])

    actx.close()
    barT, barV, barS, barG = p.c["T"], p.c["V"], p.c["S"], p.c["G"]
    barDY = p.c["DY"]
    bar = [("T", barT), ("V", barV), ("S", barS), ("G", barG)]

    fctx = ExitStack()
    FE = fctx.enter_context
    ystage = FE(nc.sbuf_tensor("ystage", [128, 2 * D], F32))
    big48 = FE(nc.sbuf_tensor("big48", [128, 24576], BF16))
    big32 = FE(nc.sbuf_tensor("big32", [128, 16384], BF16))
    w1wo = FE(nc.sbuf_tensor("w1wo", [128, 8192], BF16))
    otbuf = FE(nc.sbuf_tensor("otbuf", [128, 4096], BF16))
    alpha_bf = FE(nc.sbuf_tensor("alpha_bf", [128, NT * NSL], BF16))
    alphaT = FE(nc.sbuf_tensor("alphaT", [64, S], BF16))
    fsm = FE(nc.sbuf_tensor("fsmall", [128, 256], F32))
    maskb = FE(nc.sbuf_tensor("maskb", [128, NSL], BF16))
    gh_bf = FE(nc.sbuf_tensor("gh_bf", [64, 3 * D], BF16))
    bih_bc = FE(nc.sbuf_tensor("bih_bc", [64, 3 * D], BF16))
    si_bf = FE(nc.sbuf_tensor("si_bf", [64, D], BF16))
    siT = FE(nc.sbuf_tensor("siT", [128, NJ * NSL], BF16))
    gtmp = FE(nc.sbuf_tensor("gtmp", [64, D], F32))
    gtmp2 = FE(nc.sbuf_tensor("gtmp2", [64, D], F32))
    r_sb = FE(nc.sbuf_tensor("r_sb", [64, D], F32))
    z_sb = FE(nc.sbuf_tensor("z_sb", [64, D], F32))
    snew_f = FE(nc.sbuf_tensor("snew_f", [64, D], F32))
    snew_bf = FE(nc.sbuf_tensor("snew_bf", [64, D], BF16))
    snewT = FE(nc.sbuf_tensor("snewT", [128, NJ * NSL], BF16))
    q2_bf = FE(nc.sbuf_tensor("q2_bf", [64, DQ], BF16))
    k2_bf = FE(nc.sbuf_tensor("k2_bf", [64, DQ], BF16))
    qT2 = FE(nc.sbuf_tensor("qT2", [128, 2 * NSL], BF16))
    kT2 = FE(nc.sbuf_tensor("kT2", [128, 2 * NSL], BF16))
    vs_sb = FE(nc.sbuf_tensor("vs_sb", [64, D], BF16))
    e2_sb = FE(nc.sbuf_tensor("e2_sb", [64, NSL], F32))
    aP_bf = FE(nc.sbuf_tensor("aP_bf", [64, NSL], BF16))
    aPT = FE(nc.sbuf_tensor("aPT", [64, NSL], BF16))
    y2n_bf = FE(nc.sbuf_tensor("y2n_bf", [64, D], BF16))
    y2nT = FE(nc.sbuf_tensor("y2nT", [128, NJ * NSL], BF16))
    sv_bf = FE(nc.sbuf_tensor("sv_bf", [64, D], BF16))
    fo_sb = FE(nc.sbuf_tensor("fo_sb", [128, D], F32))
    fscr = FE(nc.sbuf_tensor("fscr", [128, D], BF16))
    bo_bc = FE(nc.sbuf_tensor("bo_bc", [128, D], BF16))
    b2rt_bc = FE(nc.sbuf_tensor("b2rt_bc", [128, NSL], F32))
    lng_bc = FE(nc.sbuf_tensor("lng_bc", [64, D], BF16))
    lnb_bc = FE(nc.sbuf_tensor("lnb_bc", [64, D], BF16))
    bq_bc = FE(nc.sbuf_tensor("bq_bc", [64, DQ], BF16))
    bk_bc = FE(nc.sbuf_tensor("bk_bc", [64, DQ], BF16))
    bv_bc = FE(nc.sbuf_tensor("bv_bc", [64, D], BF16))
    bvp_bc = FE(nc.sbuf_tensor("bvp_bc", [64, D], BF16))
    b1c1_rowb = FE(nc.sbuf_tensor("b1c1_rowb", [1, DH], BF16))
    wq_sb = FE(nc.sbuf_tensor("wq_sb", [128, NJ * DQ], BF16))
    wk_sb = FE(nc.sbuf_tensor("wk_sb", [128, NJ * DQ], BF16))
    w2_sb = FE(nc.sbuf_tensor("w2_sb", [128, 4 * NSL], BF16))
    ynT = big48[:, 0:16384]
    hT = big48[:, 16384:24576]
    yn_sb = big32[:, 0:16384]
    y2_f = gtmp
    wv_sb = big32[:, 0:8192]
    wvp_sb = big32[:, 8192:16384]

    p.Gdma(gh_bf[:], gh_d[:], waits=bar + [("DG", dg_gh_store)])
    dg_ghload = p.c["DG"]
    p.Gdma(bih_bc[:], dt["bihc"][:])
    g_bih = p.c["G"]
    p.Gdma(bo_bc[:], dt["boc"][:])
    p.Gdma(lng_bc[:], dt["lngc"][:])
    p.Gdma(lnb_bc[:], dt["lnbc"][:])
    p.Gdma(bq_bc[:], dt["bqc"][:])
    p.Gdma(bk_bc[:], dt["bkc"][:])
    p.Gdma(bv_bc[:], dt["bvc"][:])
    p.Gdma(bvp_bc[:], dt["bvpc"][:])
    p.Gdma(b2rt_bc[:], dt["b2bc"][:])
    dg_b2 = p.c["DG"]
    p.V(lambda e: e.tensor_scalar(out=b2rt_bc[:], in0=b2rt_bc[:],
                                  scalar1=rt_col[:], scalar2=None, op0=ALU.mult),
        inc=True, waits=[("DG", dg_b2), ("V", v_rt), ("V", barV)])
    v_b2rt = p.c["V"]
    g_after_bcasts = p.c["G"]

    dgw1 = p.Gdma(w1wo[:, 0:4096], dt["w1"][0:1024, :].rearrange("(c p) f -> p c f", p=128),
                  waits=bar)
    dgw2 = p.Gdma(w2_sb[:], dt["w2"][:].rearrange("(c p) f -> p c f", p=128))
    dgwq = p.Gdma(wq_sb[:], dt["wq"][:].rearrange("(c p) f -> p c f", p=128))
    dgwk = p.Gdma(wk_sb[:], dt["wk"][:].rearrange("(c p) f -> p c f", p=128))

    p.Gdma(otbuf[:], dt["w1"][1024:2048, :].rearrange("(c p) f -> p c f", p=128),
           waits=bar)
    dgw1b = p.c["DG"]
    p.T(lambda e: e.matmul(out=pv[0:1, 0:512], lhsT=onescol[0:64, 0:1],
                           rhs=si0b[:, 0:512], start=True, stop=True), waits=bar)
    p.T(lambda e: e.matmul(out=pv[0:1, 512:1024], lhsT=onescol[0:64, 0:1],
                           rhs=si0b[:, 512:1024], start=True, stop=True), inc=True)
    tsm = p.c["T"]
    p.V(lambda e: e.tensor_scalar(out=rowstage[0:1, 0:D], in0=pv[0:1, 0:1024],
                                  scalar1=1.0 / NSL, scalar2=None, op0=ALU.mult),
        inc=True, waits=[("T", tsm), ("G", g_bih)])
    vsm = p.c["V"]
    for jc in range(NJ):
        p.T(lambda e, jc=jc: e.matmul(
                out=pv[0:128, jc:jc + 1], lhsT=rowstage[0:1, jc * 128:(jc + 1) * 128],
                rhs=onescol[0:1, 0:1], start=True, stop=True),
            inc=(jc == NJ - 1), waits=[("V", vsm)] if jc == 0 else ())
    tsmT = p.c["T"]
    p.V(lambda e: e.tensor_copy(out=siT[:, 0:NJ], in_=pv[0:128, 0:NJ]), inc=True,
        waits=[("T", tsmT)])
    vsmT = p.c["V"]
    for j in range(NJ):
        p.T(lambda e, j=j: e.matmul(
                out=pv[0:1, 0:512], lhsT=siT[:, j:j + 1],
                rhs=otbuf[:, j * 512:(j + 1) * 512],
                start=(j == 0), stop=(j == NJ - 1)),
            inc=(j == NJ - 1),
            waits=[("V", vsmT), ("DG", dgw1b)] if j == 0 else ())
    tc1 = p.c["T"]
    p.Gdma(rowf[0:1, 0:DH], dt["b1r"][:], waits=[("G", p.c["G"])])
    dgb1 = p.c["DG"]
    p.V(lambda e: e.tensor_add(out=rowf[0:1, 0:DH], in0=pv[0:1, 0:512],
                               in1=rowf[0:1, 0:DH]),
        waits=[("T", tc1), ("DG", dgb1)])
    p.V(lambda e: e.tensor_copy(out=b1c1_rowb[0:1, :], in_=rowf[0:1, 0:DH]), inc=True)
    vb1 = p.c["V"]
    for j in range(4):
        p.T(lambda e, j=j: e.matmul(
                out=pv[0:128, NJ + j:NJ + j + 1], lhsT=b1c1_rowb[0:1, j * 128:(j + 1) * 128],
                rhs=onescol[0:1, 0:1], start=True, stop=True),
            inc=(j == 3), waits=[("V", vb1)] if j == 0 else ())
    tb1 = p.c["T"]
    p.V(lambda e: e.tensor_copy(out=b1c1_col[:], in_=pv[0:128, NJ:NJ + 4]), inc=True,
        waits=[("T", tb1)])
    v_b1c1 = p.c["V"]

    ystage_cons = [0, 0]
    for b in range(BL):
        if b == 1:
            dgw1 = p.Gdma(w1wo[:, 0:4096], dt["w1"][0:1024, :].rearrange("(c p) f -> p c f", p=128),
                          waits=[("T", p.c["T"])])
        tr_done = p.c["T"]
        guardT = barT if b == 0 else p.c["T"]
        guardS = barS if b == 0 else p.c["S"]
        for tt in range(NT):
            sl = tt % 2
            dy = p.Ydma(ystage[:, sl * D:(sl + 1) * D],
                        y_d[b * S + tt * 128: b * S + (tt + 1) * 128, :],
                        waits=[("V", ystage_cons[sl]), ("DY", barDY)])
            ys = ystage[:, sl * D:(sl + 1) * D]
            idx = b * NT + tt
            p.V(lambda e, ys=ys: e.bn_stats(out=bnst[:, 0:6], in_=ys[:, 0:512]),
                waits=([("DY", dy)] + ([("T", guardT), ("S", guardS)] if tt == 0 else [])))
            p.V(lambda e, ys=ys: e.bn_stats(out=bnst[:, 6:12], in_=ys[:, 512:1024]),
                inc=True)
            vbs = p.c["V"]
            p.V(lambda e: e.bn_aggr(out=bnagg[:], in_=bnst[:]), inc=True,
                waits=[("V", vbs)])
            vv = p.c["V"]
            p.Sc(lambda e: e.activation(out=cols[:, 4:5], in_=bnagg[:, 1:2],
                                        func=ACT.Ln, bias=epscol[:, 0:1], scale=1.0),
                 inc=True, waits=[("V", vv)])
            sl1 = p.c["S"]
            p.Sc(lambda e: e.activation(out=cols[:, 5:6], in_=cols[:, 4:5],
                                        func=ACT.Exp, scale=-0.5), inc=True,
                 waits=[("S", sl1)])
            ss = p.c["S"]
            p.V(lambda e, ys=ys, tt=tt: e.tensor_scalar(
                    out=yn_sb[:, tt * D:(tt + 1) * D], in0=ys,
                    scalar1=bnagg[:, 0:1], scalar2=cols[:, 5:6],
                    op0=ALU.subtract, op1=ALU.mult),
                inc=True, waits=[("S", ss)])
            ystage_cons[sl] = p.c["V"]
            vyn = p.c["V"]
            for jc in range(NJ):
                p.T(lambda e, tt=tt, jc=jc: e.matmul(out=pv2[:, jc * 128:(jc + 1) * 128], lhsT=yn_sb[:, tt * D + jc * 128: tt * D + (jc + 1) * 128], rhs=identb[:], start=True, stop=True),
                    inc=(jc == NJ - 1),
                    waits=[("V", vyn), ("T", tr_done)] if jc == 0 else ())
            tp = p.c["T"]
            p.V(lambda e, tt=tt: e.tensor_copy(
                    out=ynT.rearrange("p (c w) -> p c w", w=S)[:, :, tt * 128:(tt + 1) * 128],
                    in_=pv2[:, 0:1024]), inc=True, waits=[("T", tp)])
            tr_done = p.c["T"]
        v_ynT = p.c["V"]
        if dbg and b == 0:
            p.Gdma(taps["t_yn"][:], yn_sb, waits=[("V", v_ynT)])

        evacB1 = {i: p.c["S"] for i in range(4)}
        for uc in range(4):
            for tc in range(4):
                qtr = (uc * 4 + tc) % 4
                w_ = [("V", v_ynT), ("V", v_b1c1), ("S", evacB1[qtr]), ("DG", dgw1)]
                for j in range(NJ):
                    p.T(lambda e, uc=uc, tc=tc, j=j, q_=qtr: e.matmul(
                            out=psS[:, q_ * 512:(q_ + 1) * 512],
                            lhsT=w1wo[:, j * 512 + uc * 128: j * 512 + uc * 128 + 128],
                            rhs=ynT[:, j * S + tc * 512: j * S + (tc + 1) * 512],
                            start=(j == 0), stop=(j == NJ - 1)),
                        inc=(j == NJ - 1), waits=w_ if j == 0 else ())
                tmm = p.c["T"]
                p.Sc(lambda e, uc=uc, tc=tc, q_=qtr: e.activation(
                        out=hT[:, uc * S + tc * 512: uc * S + (tc + 1) * 512],
                        in_=psS[:, q_ * 512:(q_ + 1) * 512], func=ACT.Gelu,
                        bias=b1c1_col[:, uc:uc + 1], scale=1.0),
                     inc=True, waits=[("T", tmm)])
                evacB1[qtr] = p.c["S"]
        s_hT = p.c["S"]
        t_B1 = p.c["T"]
        if dbg and b == 0:
            p.Gdma(taps["t_hT"][:], hT, waits=[("S", s_hT)])
        dgwo = p.Gdma(w1wo[:], dt["wo"][:].rearrange("(c p) f -> p c f", p=128),
                      waits=[("T", t_B1)])
        wo_sb = w1wo

        for tt in range(NT):
            w_ = [("S", s_hT), ("DG", dgw2), ("V", max(p.c["V"], v_b2rt))]
            for uc in range(4):
                p.T(lambda e, tt=tt, uc=uc: e.matmul(
                        out=pv[0:128, 0:NSL],
                        lhsT=hT[:, uc * S + tt * 128: uc * S + (tt + 1) * 128],
                        rhs=w2_sb[:, uc * NSL:(uc + 1) * NSL],
                        start=(uc == 0), stop=(uc == 3)),
                    inc=(uc == 3), waits=w_ if uc == 0 else ())
            trt = p.c["T"]
            p.V(lambda e: e.scalar_tensor_tensor(
                    out=fsm[:, 0:NSL], in0=pv[0:128, 0:NSL], scalar=rt_col[:],
                    in1=b2rt_bc[:], op0=ALU.mult, op1=ALU.add),
                inc=True, waits=[("T", trt)])
            p.V(lambda e: e.max(out=fsm[:, 64:72], in_=fsm[:, 0:NSL]), inc=True)
            vm8 = p.c["V"]
            p.V(lambda e: e.tensor_scalar(out=cols[:, 8:9], in0=fsm[:, 64:65],
                                          scalar1=-1.0, scalar2=None, op0=ALU.mult),
                inc=True, waits=[("V", vm8)])
            vneg = p.c["V"]
            p.Sc(lambda e: e.activation(out=fsm[:, 128:192], in_=fsm[:, 0:NSL],
                                        func=ACT.Exp, bias=cols[:, 8:9], scale=1.0),
                 inc=True, waits=[("V", vneg)])
            sexp = p.c["S"]
            p.V(lambda e: e.tensor_single_scalar(
                    out=maskb[:], in_=fsm[:, 0:NSL], scalar=fsm[:, 71:72], op=ALU.is_ge),
                waits=[("S", sexp)])
            p.V(lambda e: e.scalar_tensor_tensor(
                    out=fsm[:, 192:256], in0=fsm[:, 128:192], scalar=1.0, in1=maskb[:],
                    op0=ALU.bypass, op1=ALU.mult, accum_out=cols[:, 9:10]), inc=True)
            vem = p.c["V"]
            p.Sc(lambda e: e.activation(out=cols[:, 15:16], in_=cols[:, 9:10],
                                        func=ACT.Ln), inc=True, waits=[("V", vem)])
            srr0 = p.c["S"]
            p.Sc(lambda e: e.activation(out=cols[:, 10:11], in_=cols[:, 15:16],
                                        func=ACT.Exp, scale=-1.0), inc=True,
                 waits=[("S", srr0)])
            srr = p.c["S"]
            p.V(lambda e, tt=tt: e.tensor_scalar(
                    out=alpha_bf[:, tt * NSL:(tt + 1) * NSL], in0=fsm[:, 192:256],
                    scalar1=cols[:, 10:11], scalar2=None, op0=ALU.mult),
                inc=True, waits=[("S", srr)])
            va = p.c["V"]
            p.T(lambda e, tt=tt: e.matmul(out=pv2[0:64, (tt % 8) * 128:(tt % 8) * 128 + 128], lhsT=alpha_bf[:, tt * NSL:(tt + 1) * NSL], rhs=identb[:], start=True, stop=True),
                inc=True, waits=[("V", va)])
            ta = p.c["T"]
            p.V(lambda e, tt=tt: e.tensor_copy(
                    out=alphaT[0:64, tt * 128:(tt + 1) * 128],
                    in_=pv2[0:64, (tt % 8) * 128:(tt % 8) * 128 + 128]),
                inc=True, waits=[("T", ta)])
        v_alpha = p.c["V"]
        t_B2 = p.c["T"]
        if dbg and b == 0:
            p.Gdma(taps["t_alpha"][:], alpha_bf[:], waits=[("V", v_alpha)])
        dgwih = p.Gdma(big48[:], dt["WihT"][:].rearrange("(c p) f -> p c f", p=128),
                       waits=[("T", t_B2)])
        wih_sb = big48

        for tt in range(NT):
            w_ = [("V", v_alpha), ("S", p.c["S"]), ("V", p.c["V"])]
            p.T(lambda e, tt=tt: e.matmul(
                    out=psS[0:NSL, 0:512],
                    lhsT=alpha_bf[:, tt * NSL:(tt + 1) * NSL],
                    rhs=yn_sb[:, tt * D: tt * D + 512],
                    start=(tt == 0), stop=(tt == NT - 1)),
                waits=w_ if tt == 0 else ())
            p.T(lambda e, tt=tt: e.matmul(
                    out=psS[0:NSL, 512:1024],
                    lhsT=alpha_bf[:, tt * NSL:(tt + 1) * NSL],
                    rhs=yn_sb[:, tt * D + 512: tt * D + 1024],
                    start=(tt == 0), stop=(tt == NT - 1)))
            p.T(lambda e, tt=tt: e.matmul(
                    out=psS[0:NSL, 1024:1025],
                    lhsT=alpha_bf[:, tt * NSL:(tt + 1) * NSL],
                    rhs=onescol[:],
                    start=(tt == 0), stop=(tt == NT - 1)),
                inc=(tt == NT - 1))
        t_si = p.c["T"]
        p.V(lambda e: e.tensor_scalar_add(cols[0:NSL, 11:12],
                                          psS[0:NSL, 1024:1025], 1e-8),
            inc=True, waits=[("T", t_si)])
        vdn0 = p.c["V"]
        p.Sc(lambda e: e.activation(out=cols[0:NSL, 13:14], in_=cols[0:NSL, 11:12],
                                    func=ACT.Ln), inc=True, waits=[("V", vdn0)])
        sdn0 = p.c["S"]
        p.Sc(lambda e: e.activation(out=cols[0:NSL, 12:13], in_=cols[0:NSL, 13:14],
                                    func=ACT.Exp, scale=-1.0), inc=True,
             waits=[("S", sdn0)])
        sdn = p.c["S"]
        p.V(lambda e: e.tensor_scalar(
                out=si_bf[:], in0=psS[0:NSL, 0:1024], scalar1=cols[0:NSL, 12:13],
                scalar2=None, op0=ALU.mult), inc=True, waits=[("S", sdn)])
        v_si = p.c["V"]
        if dbg and b == 0:
            p.Gdma(taps["t_si"][:], si_bf[:], waits=[("V", v_si)])
        for jc in range(NJ):
            p.T(lambda e, jc=jc: e.matmul(out=pv2[:, jc * 64:(jc + 1) * 64], lhsT=si_bf[:, jc * 128:(jc + 1) * 128], rhs=identb[0:NSL, 0:NSL], start=True, stop=True),
                inc=(jc == NJ - 1), waits=[("V", v_si)] if jc == 0 else ())
        tsiT = p.c["T"]
        p.V(lambda e: e.tensor_copy(out=siT[:], in_=pv2[:, 0:NJ * 64]),
            inc=True, waits=[("T", tsiT)])
        v_siT = p.c["V"]
        dgwv = p.Gdma(wv_sb, dt["wv"][:].rearrange("(c p) f -> p c f", p=128)
                      .rearrange("p c f -> p (c f)") if False else
                      dt["wv"][:].rearrange("(c p) f -> p c f", p=128),
                      waits=[("T", t_si)])
        dgwvp = p.Gdma(wvp_sb, dt["wvp"][:].rearrange("(c p) f -> p c f", p=128))

        def gate_mm(g, w_extra):
            for hf in range(2):
                for dc in range(NJ):
                    p.T(lambda e, g=g, hf=hf, dc=dc: e.matmul(
                            out=psS[0:NSL, hf * 512:(hf + 1) * 512],
                            lhsT=siT[:, dc * NSL:(dc + 1) * NSL],
                            rhs=wih_sb[:, dc * 3 * D + g * D + hf * 512:
                                       dc * 3 * D + g * D + hf * 512 + 512],
                            start=(dc == 0), stop=(dc == NJ - 1)),
                        inc=(hf == 1 and dc == NJ - 1),
                        waits=w_extra if (hf == 0 and dc == 0) else ())
        gate_mm(0, [("V", v_siT), ("DG", dgwih)])
        t_g = p.c["T"]
        p.V(lambda e: e.tensor_add(out=gtmp[:], in0=psS[0:NSL, 0:1024],
                                   in1=gh_bf[:, 0:D]),
            waits=[("T", t_g), ("DG", dg_ghload), ("DG", g_bih)])
        p.V(lambda e: e.tensor_add(out=gtmp[:], in0=gtmp[:], in1=bih_bc[:, 0:D]),
            inc=True)
        vg = p.c["V"]
        p.Sc(lambda e: e.activation(out=r_sb[:], in_=gtmp[:], func=ACT.Sigmoid),
             inc=True, waits=[("V", vg)])
        s_r = p.c["S"]
        gate_mm(1, [("V", vg)])
        t_g = p.c["T"]
        p.V(lambda e: e.tensor_add(out=gtmp[:], in0=psS[0:NSL, 0:1024],
                                   in1=gh_bf[:, D:2 * D]),
            waits=[("T", t_g), ("S", s_r)])
        p.V(lambda e: e.tensor_add(out=gtmp[:], in0=gtmp[:], in1=bih_bc[:, D:2 * D]),
            inc=True)
        vg = p.c["V"]
        p.Sc(lambda e: e.activation(out=z_sb[:], in_=gtmp[:], func=ACT.Sigmoid),
             inc=True, waits=[("V", vg)])
        s_z = p.c["S"]
        gate_mm(2, [("V", vg)])
        t_g = p.c["T"]
        p.V(lambda e: e.tensor_mul(out=gtmp2[:], in0=r_sb[:], in1=gh_bf[:, 2 * D:3 * D]),
            waits=[("T", t_g), ("S", s_r)])
        p.V(lambda e: e.tensor_add(out=gtmp2[:], in0=gtmp2[:], in1=psS[0:NSL, 0:1024]))
        p.V(lambda e: e.tensor_add(out=gtmp2[:], in0=gtmp2[:], in1=bih_bc[:, 2 * D:3 * D]),
            inc=True)
        vg = p.c["V"]
        p.Sc(lambda e: e.activation(out=gtmp2[:], in_=gtmp2[:], func=ACT.Tanh),
             inc=True, waits=[("V", vg)])
        s_n = p.c["S"]
        p.V(lambda e: e.tensor_sub(out=gtmp[:], in0=si0b[:], in1=gtmp2[:]),
            waits=[("S", s_n), ("S", s_z)])
        p.V(lambda e: e.tensor_mul(out=gtmp[:], in0=z_sb[:], in1=gtmp[:]))
        p.V(lambda e: e.tensor_add(out=snew_f[:], in0=gtmp2[:], in1=gtmp[:]))
        p.V(lambda e: e.tensor_copy(out=snew_bf[:], in_=snew_f[:]), inc=True)
        v_sn = p.c["V"]
        if dbg and b == 0:
            p.Gdma(taps["t_snew"][:], snew_f[:], waits=[("V", v_sn)])
        for jc in range(NJ):
            p.T(lambda e, jc=jc: e.matmul(out=pv2[:, jc * 64:(jc + 1) * 64], lhsT=snew_bf[:, jc * 128:(jc + 1) * 128], rhs=identb[0:NSL, 0:NSL], start=True, stop=True),
                inc=(jc == NJ - 1), waits=[("V", v_sn)] if jc == 0 else ())
        tsnT = p.c["T"]
        p.V(lambda e: e.tensor_copy(out=snewT[:], in_=pv2[:, 0:NJ * 64]),
            inc=True, waits=[("T", tsnT)])
        v_snT = p.c["V"]

        for dc in range(NJ):
            p.T(lambda e, dc=dc: e.matmul(
                    out=pv[0:NSL, 0:DQ], lhsT=snewT[:, dc * NSL:(dc + 1) * NSL],
                    rhs=wq_sb[:, dc * DQ:(dc + 1) * DQ],
                    start=(dc == 0), stop=(dc == NJ - 1)),
                waits=[("V", v_snT), ("DG", dgwq)] if dc == 0 else ())
        for dc in range(NJ):
            p.T(lambda e, dc=dc: e.matmul(
                    out=pv[0:NSL, 512:512 + DQ], lhsT=snewT[:, dc * NSL:(dc + 1) * NSL],
                    rhs=wk_sb[:, dc * DQ:(dc + 1) * DQ],
                    start=(dc == 0), stop=(dc == NJ - 1)),
                inc=(dc == NJ - 1), waits=[("DG", dgwk)] if dc == 0 else ())
        tqk = p.c["T"]
        p.V(lambda e: e.tensor_add(out=q2_bf[:], in0=pv[0:NSL, 0:DQ], in1=bq_bc[:]),
            waits=[("T", tqk), ("DG", dg_b2)])
        p.V(lambda e: e.tensor_add(out=k2_bf[:], in0=pv[0:NSL, 512:512 + DQ],
                                   in1=bk_bc[:]), inc=True)
        vqk = p.c["V"]
        for c2 in range(2):
            p.T(lambda e, c2=c2: e.matmul(out=pv2[:, c2 * 64:(c2 + 1) * 64], lhsT=q2_bf[:, c2 * 128:(c2 + 1) * 128], rhs=identb[0:NSL, 0:NSL], start=True, stop=True),
                waits=[("V", vqk)] if c2 == 0 else ())
        for c2 in range(2):
            p.T(lambda e, c2=c2: e.matmul(out=pv2[:, 128 + c2 * 64:128 + (c2 + 1) * 64], lhsT=k2_bf[:, c2 * 128:(c2 + 1) * 128], rhs=identb[0:NSL, 0:NSL], start=True, stop=True),
                inc=(c2 == 1))
        tqkT = p.c["T"]
        p.V(lambda e: e.tensor_copy(out=qT2[:], in_=pv2[:, 0:128]), waits=[("T", tqkT)])
        p.V(lambda e: e.tensor_copy(out=kT2[:], in_=pv2[:, 128:256]), inc=True)
        vqkT = p.c["V"]
        for c2 in range(2):
            p.T(lambda e, c2=c2: e.matmul(
                    out=pv[0:NSL, 768:768 + NSL], lhsT=qT2[:, c2 * NSL:(c2 + 1) * NSL],
                    rhs=kT2[:, c2 * NSL:(c2 + 1) * NSL],
                    start=(c2 == 0), stop=(c2 == 1)),
                inc=(c2 == 1), waits=[("V", vqkT)] if c2 == 0 else ())
        tsc2 = p.c["T"]
        p.V(lambda e: e.reduce_max(out=cols[0:NSL, 13:14],
                                   in_=pv[0:NSL, 768:768 + NSL], axis=AXX),
            inc=True, waits=[("T", tsc2)])
        vacc = p.c["V"]
        p.V(lambda e: e.tensor_scalar(out=cols[0:NSL, 14:15], in0=cols[0:NSL, 13:14],
                                      scalar1=-0.0625, scalar2=None, op0=ALU.mult),
            inc=True, waits=[("V", vacc)])
        vng = p.c["V"]
        p.Sc(lambda e: e.activation(out=e2_sb[:], in_=pv[0:NSL, 768:768 + NSL],
                                    func=ACT.Exp, scale=0.0625,
                                    bias=cols[0:NSL, 14:15],
                                    accum_out=cols[0:NSL, 15:16]),
             inc=True, waits=[("V", vng)])
        se2 = p.c["S"]
        p.Sc(lambda e: e.activation(out=cols[0:NSL, 7:8], in_=cols[0:NSL, 15:16],
                                    func=ACT.Ln), inc=True, waits=[("S", se2)])
        sr20 = p.c["S"]
        p.Sc(lambda e: e.activation(out=cols[0:NSL, 6:7], in_=cols[0:NSL, 7:8],
                                    func=ACT.Exp, scale=-1.0), inc=True,
             waits=[("S", sr20)])
        sr2e = p.c["S"]
        p.V(lambda e: e.tensor_scalar(out=aP_bf[:], in0=e2_sb[:],
                                      scalar1=cols[0:NSL, 6:7], scalar2=None,
                                      op0=ALU.mult), inc=True, waits=[("S", sr2e)])
        vaP = p.c["V"]
        p.T(lambda e: e.matmul(out=pv2[0:NSL, 0:NSL], lhsT=aP_bf[:], rhs=identb[0:NSL, 0:NSL], start=True, stop=True), inc=True,
            waits=[("V", vaP)])
        taP = p.c["T"]
        p.V(lambda e: e.tensor_copy(out=aPT[:], in_=pv2[0:NSL, 0:NSL]),
            inc=True, waits=[("T", taP)])
        vaPT = p.c["V"]
        for hf in range(2):
            for dc in range(NJ):
                p.T(lambda e, hf=hf, dc=dc: e.matmul(
                        out=psS[0:NSL, hf * 512:(hf + 1) * 512],
                        lhsT=snewT[:, dc * NSL:(dc + 1) * NSL],
                        rhs=wv_sb[:, dc * D + hf * 512: dc * D + hf * 512 + 512],
                        start=(dc == 0), stop=(dc == NJ - 1)),
                    inc=(hf == 1 and dc == NJ - 1),
                    waits=[("DG", dgwv), ("V", v_sn)] if (hf == 0 and dc == 0) else ())
        tvs = p.c["T"]
        p.V(lambda e: e.tensor_add(out=vs_sb[:], in0=psS[0:NSL, 0:1024], in1=bv_bc[:]),
            inc=True, waits=[("T", tvs)])
        vvs = p.c["V"]
        for hf in range(2):
            p.T(lambda e, hf=hf: e.matmul(
                    out=pv[0:NSL, hf * 512:(hf + 1) * 512], lhsT=aPT[:],
                    rhs=vs_sb[:, hf * 512:(hf + 1) * 512], start=True, stop=True),
                inc=(hf == 1), waits=[("V", vvs), ("V", vaPT)] if hf == 0 else ())
        tcm = p.c["T"]
        p.V(lambda e: e.tensor_add(out=y2_f[:], in0=pv[0:NSL, 0:1024], in1=snew_f[:]),
            waits=[("T", tcm)])
        p.V(lambda e: e.bn_stats(out=bnst[0:NSL, 0:6], in_=y2_f[:, 0:512]))
        p.V(lambda e: e.bn_stats(out=bnst[0:NSL, 6:12], in_=y2_f[:, 512:1024]), inc=True)
        vbs = p.c["V"]
        p.V(lambda e: e.bn_aggr(out=bnagg[0:NSL, :], in_=bnst[0:NSL, :]), inc=True,
            waits=[("V", vbs)])
        vst2 = p.c["V"]
        p.Sc(lambda e: e.activation(out=cols[0:NSL, 5:6], in_=bnagg[0:NSL, 1:2],
                                    func=ACT.Ln, bias=epscol[0:NSL, 0:1], scale=1.0),
             inc=True, waits=[("V", vst2)])
        sl1 = p.c["S"]
        p.Sc(lambda e: e.activation(out=cols[0:NSL, 6:7], in_=cols[0:NSL, 5:6],
                                    func=ACT.Exp, scale=-0.5), inc=True,
             waits=[("S", sl1)])
        sr2 = p.c["S"]
        p.V(lambda e: e.tensor_scalar(
                out=y2_f[:], in0=y2_f[:], scalar1=bnagg[0:NSL, 0:1],
                scalar2=cols[0:NSL, 6:7], op0=ALU.subtract, op1=ALU.mult),
            waits=[("S", sr2)])
        p.V(lambda e: e.tensor_mul(out=y2_f[:], in0=y2_f[:], in1=lng_bc[:]))
        p.V(lambda e: e.tensor_add(out=y2n_bf[:], in0=y2_f[:], in1=lnb_bc[:]),
            inc=True)
        v_y2 = p.c["V"]
        if dbg and b == 0:
            p.Gdma(taps["t_y2n"][:], y2n_bf[:], waits=[("V", v_y2)])
        for jc in range(NJ):
            p.T(lambda e, jc=jc: e.matmul(out=pv2[:, jc * 64:(jc + 1) * 64], lhsT=y2n_bf[:, jc * 128:(jc + 1) * 128], rhs=identb[0:NSL, 0:NSL], start=True, stop=True),
                inc=(jc == NJ - 1), waits=[("V", v_y2)] if jc == 0 else ())
        ty2T = p.c["T"]
        p.V(lambda e: e.tensor_copy(out=y2nT[:], in_=pv2[:, 0:NJ * 64]),
            inc=True, waits=[("T", ty2T)])
        v_y2T = p.c["V"]
        for hf in range(2):
            for dc in range(NJ):
                p.T(lambda e, hf=hf, dc=dc: e.matmul(
                        out=psS[0:NSL, hf * 512:(hf + 1) * 512],
                        lhsT=y2nT[:, dc * NSL:(dc + 1) * NSL],
                        rhs=wvp_sb[:, dc * D + hf * 512: dc * D + hf * 512 + 512],
                        start=(dc == 0), stop=(dc == NJ - 1)),
                    inc=(hf == 1 and dc == NJ - 1),
                    waits=[("DG", dgwvp), ("V", v_y2T)] if (hf == 0 and dc == 0) else ())
        tsv = p.c["T"]
        p.V(lambda e: e.tensor_add(out=sv_bf[:], in0=psS[0:NSL, 0:1024], in1=bvp_bc[:]),
            inc=True, waits=[("T", tsv)])
        v_sv = p.c["V"]
        if dbg and b == 0:
            p.Gdma(taps["t_sv"][:], sv_bf[:], waits=[("V", v_sv)])

        v_grp = p.c["V"]
        s_ot = p.c["S"]
        for tg in range(4):
            for dc in range(NJ):
                qtr = dc % 4
                w_ = [("V", v_sv), ("V", v_alpha), ("V", v_grp), ("S", s_ot)]
                p.T(lambda e, dc=dc, tg=tg, q_=qtr: e.matmul(
                        out=psS[:, q_ * 512:(q_ + 1) * 512],
                        lhsT=sv_bf[0:NSL, dc * 128:(dc + 1) * 128],
                        rhs=alphaT[0:NSL, tg * 512:(tg + 1) * 512],
                        start=True, stop=True),
                    inc=(qtr == 3 or dc == NJ - 1),
                    waits=w_ if dc == 0 else ([("S", s_ot)] if dc == 4 else ()))
                if qtr == 3 or dc == NJ - 1:
                    tmm = p.c["T"]
                    base = (dc // 4) * 4
                    nq = dc - base + 1
                    p.Sc(lambda e, base=base, nq=nq: e.copy(
                            out=otbuf[:, base * 512:(base + nq) * 512],
                            in_=psS[:, 0:nq * 512] if base == 0 else
                                psS[:, 0:nq * 512]),
                         inc=True, waits=[("T", tmm)])
                    s_ot = p.c["S"]
            for u in range(4):
                tt = tg * 4 + u
                sl = tt % 2
                dy = p.Ydma(ystage[:, sl * D:(sl + 1) * D],
                            y_d[b * S + tt * 128: b * S + (tt + 1) * 128, :],
                            waits=[("V", ystage_cons[sl])])
                w_ = [("S", s_ot), ("DG", dgwo), ("V", p.c["V"])]
                for hf in range(2):
                    for dc in range(NJ):
                        p.T(lambda e, u=u, hf=hf, dc=dc, sl=sl: e.matmul(
                                out=psS[:, sl * 1024 + hf * 512:
                                        sl * 1024 + hf * 512 + 512],
                                lhsT=otbuf[:, dc * 512 + u * 128: dc * 512 + u * 128 + 128],
                                rhs=wo_sb[:, dc * D + hf * 512: dc * D + hf * 512 + 512],
                                start=(dc == 0), stop=(dc == NJ - 1)),
                            inc=(hf == 1 and dc == NJ - 1),
                            waits=w_ if (hf == 0 and dc == 0) else ())
                tmm = p.c["T"]
                p.V(lambda e, sl=sl: e.scalar_tensor_tensor(
                        out=fo_sb[:], in0=psS[:, sl * 1024:(sl + 1) * 1024], scalar=1.0,
                        in1=bo_bc[:], op0=ALU.bypass, op1=ALU.add),
                    inc=True, waits=[("T", tmm), ("DG", dg_b2)])
                vfo = p.c["V"]
                p.V(lambda e: e.bn_stats(out=bnst[:, 0:6], in_=fo_sb[:, 0:512]),
                    waits=[("V", vfo)])
                p.V(lambda e: e.bn_stats(out=bnst[:, 6:12], in_=fo_sb[:, 512:1024]),
                    inc=True)
                vbs = p.c["V"]
                p.V(lambda e: e.bn_aggr(out=bnagg[:], in_=bnst[:]), inc=True,
                    waits=[("V", vbs)])
                vst3 = p.c["V"]
                p.Sc(lambda e: e.activation(out=cols[:, 5:6], in_=bnagg[:, 1:2],
                                            func=ACT.Ln, bias=epscol[:, 0:1], scale=1.0),
                     inc=True, waits=[("V", vst3)])
                sl1 = p.c["S"]
                p.Sc(lambda e: e.activation(out=cols[:, 6:7], in_=cols[:, 5:6],
                                            func=ACT.Exp, scale=-0.5), inc=True,
                     waits=[("S", sl1)])
                sr3 = p.c["S"]
                p.V(lambda e: e.tensor_scalar(
                        out=fo_sb[:], in0=fo_sb[:], scalar1=bnagg[:, 0:1],
                        scalar2=cols[:, 6:7], op0=ALU.subtract, op1=ALU.mult),
                    waits=[("S", sr3), ("DY", dy)])
                p.V(lambda e, sl=sl: e.tensor_add(
                        out=ystage[:, sl * D:(sl + 1) * D], in0=fo_sb[:],
                        in1=ystage[:, sl * D:(sl + 1) * D]), inc=True)
                vout = p.c["V"]
                p.Ydma(out_d[b * S + tt * 128: b * S + (tt + 1) * 128, :],
                       ystage[:, sl * D:(sl + 1) * D], waits=[("V", vout)])
                ystage_cons[sl] = vout
            v_grp = p.c["V"]

    fctx.close()
    p.Ywait([("DY", p.c["DY"])])

    with nc.Block() as block:
        @block.tensor
        def _(e):
            for fn in p.q["T"]:
                fn(e, sems)

        @block.vector
        def _(e):
            for fn in p.q["V"]:
                fn(e, sems)

        @block.scalar
        def _(e):
            for fn in p.q["S"]:
                fn(e, sems)

        @block.gpsimd
        def _(e):
            for fn in p.q["G"]:
                fn(e, sems)

        @block.sync
        def _(e):
            for fn in p.q["Y"]:
                fn(e, sems)

    ctx.close()
    return nc, taps


def x_d_slice(dt, b, tt):
    return dt["x"][b * S + tt * 128: b * S + (tt + 1) * 128, :]


def _prep_core_inputs(inputs, core):
    A = lambda v: np.ascontiguousarray(np.asarray(v), dtype=np.float32)
    x = np.asarray(inputs["L0_attn_x"])
    cosb, sinb = _rope_tables()
    return {
        "x": A(x[core * BL:(core + 1) * BL].reshape(BL * S, D)),
        "WqT": A(np.asarray(inputs["L0_attn_Wq"]).T),
        "WkT": A(np.asarray(inputs["L0_attn_Wk"]).T),
        "WvT": A(np.asarray(inputs["L0_attn_Wv"]).T),
        "WoT": A(np.asarray(inputs["L0_attn_Wo"]).T),
        "w1": A(inputs["L0_ffn_w1"]),
        "w2": A(inputs["L0_ffn_w2"]),
        "WihT": A(np.asarray(inputs["L0_ffn_Wih"]).T),
        "WhhT": A(np.asarray(inputs["L0_ffn_Whh"]).T),
        "wq": A(inputs["L0_ffn_wq"]),
        "wk": A(inputs["L0_ffn_wk"]),
        "wv": A(inputs["L0_ffn_wv"]),
        "wvp": A(inputs["L0_ffn_wvp"]),
        "wo": A(inputs["L0_ffn_wo"]),
        "slot_init": A(inputs["L0_ffn_slot_init"]),
        "b1r": A(np.asarray(inputs["L0_ffn_b1"])[None]),
        "b2bc": A(np.tile(np.asarray(inputs["L0_ffn_b2"])[None], (128, 1))),
        "taur": A(np.tile(np.asarray(inputs["L0_ffn_tau"]).reshape(1, 1), (128, 1))),
        "bihc": A(np.tile(np.asarray(inputs["L0_ffn_bih"])[None], (64, 1))),
        "bhhc": A(np.tile(np.asarray(inputs["L0_ffn_bhh"])[None], (64, 1))),
        "bqc": A(np.tile(np.asarray(inputs["L0_ffn_bq"])[None], (64, 1))),
        "bkc": A(np.tile(np.asarray(inputs["L0_ffn_bk"])[None], (64, 1))),
        "bvc": A(np.tile(np.asarray(inputs["L0_ffn_bv"])[None], (64, 1))),
        "lngc": A(np.tile(np.asarray(inputs["L0_ffn_ln_g"])[None], (64, 1))),
        "lnbc": A(np.tile(np.asarray(inputs["L0_ffn_ln_b"])[None], (64, 1))),
        "bvpc": A(np.tile(np.asarray(inputs["L0_ffn_bvp"])[None], (64, 1))),
        "boc": A(np.tile(np.asarray(inputs["L0_ffn_bo"])[None], (128, 1))),
        "identc": np.eye(128, dtype=np.float32),
        "cosc": cosb,
        "sinc": sinb,
        "mextc": _mext_table(),
    }


def kernel(**inputs):
    nc, _ = build_nc(dbg=False)
    in_maps = [_prep_core_inputs(inputs, c) for c in range(8)]
    res = run_bass_kernel_spmd(nc, in_maps, core_ids=list(range(8)))
    return np.concatenate(
        [np.asarray(r["out"], np.float32).reshape(BL, S, D) for r in res.results], axis=0)


# version 09
# speedup vs baseline: 1.0224x; 1.0224x over previous
import numpy as np
import ml_dtypes
from contextlib import ExitStack
import concourse.bass as bass
import concourse.mybir as mybir
from concourse.bass_utils import run_bass_kernel_spmd

F32 = mybir.dt.float32
BF16 = mybir.dt.bfloat16
ALU = mybir.AluOpType
ACT = mybir.ActivationFunctionType
AXX = mybir.AxisListType.X

BL = 2
S = 2048
D = 1024
H = 16
NT = 16
NJ = 8
QCW = 512
NQC = 4
NSL = 64
DQ = 256
DH = 512
SCALE = 0.125
LN_EPS = 1e-5

SHUF_MASK = [(i + 1 if i % 2 == 0 else i - 1) for i in range(32)]


def _rope_tables():
    HD = 64
    inv = 1.0 / (10000.0 ** (np.arange(0, HD, 2, dtype=np.float64) / HD))
    pos = np.arange(S, dtype=np.float64)
    fr = np.outer(pos, inv)
    emb = np.concatenate([fr, fr], -1)
    cos = np.cos(emb).T.astype(np.float32)
    sin = np.sin(emb).T.astype(np.float32)
    sgn = np.where(np.arange(HD) % 2 == 0, -1.0, 1.0).astype(np.float32)
    sin = sin * sgn[:, None]
    cosb = np.concatenate([cos, cos], 0)
    sinb = np.concatenate([sin, sin], 0)
    return np.ascontiguousarray(cosb), np.ascontiguousarray(sinb)


def _mext_table():
    kk = np.arange(128)[:, None]
    c = np.arange(896)[None, :]
    return np.ascontiguousarray((c >= kk + 384).astype(np.float32))


class P:

    def __init__(self, nc):
        self.nc = nc
        self.q = {k: [] for k in ("T", "V", "S", "G", "Y")}
        self.c = dict(T=0, V=0, S=0, G=0, DY=0, DG=0)

    def _emit(self, qn, fn, inc, waits):
        w = tuple(waits)
        if inc is not None:
            self.c[inc[0]] += inc[1]
        def run(e, sems, fn=fn, inc=inc, w=w):
            for snm, v in w:
                if v > 0:
                    e.wait_ge(sems[snm], v)
            ins = fn(e)
            if inc is not None and ins is not None:
                ins.then_inc(sems[inc[0]], inc[1])
        self.q[qn].append(run)

    def T(self, fn, inc=False, waits=()):
        self._emit("T", fn, ("T", 1) if inc else None, waits)

    def V(self, fn, inc=False, waits=()):
        self._emit("V", fn, ("V", 1) if inc else None, waits)

    def Sc(self, fn, inc=False, waits=()):
        self._emit("S", fn, ("S", 1) if inc else None, waits)

    def G(self, fn, inc=False, waits=()):
        self._emit("G", fn, ("G", 1) if inc else None, waits)

    def Ydma(self, out, in_, waits=()):
        self._emit("Y", lambda e, o=out, i=in_: e.dma_start(out=o, in_=i),
                   ("DY", 16), waits)
        return self.c["DY"]

    def Gdma(self, out, in_, waits=()):
        self._emit("G", lambda e, o=out, i=in_: e.dma_start(out=o, in_=i),
                   ("DG", 16), waits)
        return self.c["DG"]

    def Ywait(self, waits):
        self._emit("Y", lambda e: None, None, waits)


def build_nc(dbg=False):
    nc = bass.Bass("TRN2", target_bir_lowering=False)
    dt = {}

    def din(name, shape):
        dt[name] = nc.dram_tensor(name, list(shape), F32, kind="ExternalInput").ap()

    din("x", [BL * S, D])
    for nm, shp in [("WqT", [D, D]), ("WkT", [D, D]), ("WvT", [D, D]), ("WoT", [D, D]),
                    ("w1", [2 * D, DH]), ("w2", [DH, NSL]),
                    ("WihT", [D, 3 * D]), ("WhhT", [D, 3 * D]),
                    ("wq", [D, DQ]), ("wk", [D, DQ]), ("wv", [D, D]),
                    ("wvp", [D, D]), ("wo", [D, D]), ("slot_init", [NSL, D]),
                    ("b1r", [1, DH]), ("b2bc", [128, NSL]), ("taur", [128, 1]),
                    ("bihc", [64, 3 * D]), ("bhhc", [64, 3 * D]),
                    ("bqc", [64, DQ]), ("bkc", [64, DQ]), ("bvc", [64, D]),
                    ("lngc", [64, D]), ("lnbc", [64, D]), ("bvpc", [64, D]), ("boc", [128, D]),
                    ("identc", [128, 128]), ("cosc", [128, S]), ("sinc", [128, S]),
                    ("mextc", [128, 896])]:
        din(nm, shp)
    out_d = nc.dram_tensor("out", [BL * S, D], F32, kind="ExternalOutput").ap()
    y_d = nc.dram_tensor("y_scr", [BL * S, D], F32, kind="Internal").ap()
    gh_d = nc.dram_tensor("gh_scr", [NSL, 3 * D], F32, kind="Internal").ap()
    taps = {}
    if dbg:
        for nm, shp in [("t_qT", [128, NJ * S]), ("t_kT", [128, NJ * S]),
                        ("t_vp", [128, NT * H * 65]), ("t_attn", [128, NJ * QCW]),
                        ("t_yn", [128, NT * D]), ("t_hT", [128, 4 * S]),
                        ("t_alpha", [128, NT * NSL]), ("t_si", [NSL, D]),
                        ("t_snew", [NSL, D]), ("t_y2n", [NSL, D]), ("t_sv", [NSL, D]),
                        ("t_gh", [NSL, 3 * D]), ("t_xnT", [128, NJ * S]),
                        ("t_sx", [128, BL * NT]), ("t_dbgst", [128, 64])]:
            taps[nm] = nc.dram_tensor(nm, shp, F32, kind="ExternalOutput").ap()

    p = P(nc)
    ctx = ExitStack()
    E = ctx.enter_context

    sems = {}
    for snm in ("T", "V", "S", "G", "DY", "DG"):
        sems[snm] = E(nc.semaphore("sem" + snm))

    psS = E(nc.psum_tensor("psS", [128, 2048], F32))
    pv = E(nc.psum_tensor("pv", [128, 1024], F32))
    pv2 = E(nc.psum_tensor("pv2", [128, 1024], F32))

    identb = E(nc.sbuf_tensor("identb", [128, 128], BF16))
    onescol = E(nc.sbuf_tensor("onescol", [128, 1], BF16))
    onesrow = E(nc.sbuf_tensor("onesrow", [1, 64], BF16))
    sxbuf = E(nc.sbuf_tensor("sxbuf", [128, BL * NT], F32))
    cols = E(nc.sbuf_tensor("cols", [128, 16], F32))
    rowstage = E(nc.sbuf_tensor("rowstage", [1, D], BF16))
    rowf = E(nc.sbuf_tensor("rowf", [1, DH], F32))
    si0b = E(nc.sbuf_tensor("si0b", [NSL, D], BF16))
    si0T = E(nc.sbuf_tensor("si0T", [128, NJ * NSL], BF16))
    rt_col = E(nc.sbuf_tensor("rt_col", [128, 1], F32))
    epscol = E(nc.sbuf_tensor("epscol", [128, 1], F32))
    rt_row = E(nc.sbuf_tensor("rt_row", [1, 4], F32))
    b1c1_col = E(nc.sbuf_tensor("b1c1_col", [128, 4], F32))
    dbgst = E(nc.sbuf_tensor("dbgst", [128, 64], F32))
    bnst = E(nc.sbuf_tensor("bnst", [128, 12], F32))
    bnagg = E(nc.sbuf_tensor("bnagg", [128, 2], F32))

    p.Gdma(identb[:], dt["identc"][:])
    p.Gdma(si0b[:], dt["slot_init"][:])
    dg_si = p.c["DG"]
    p.V(lambda e: e.memset(onescol[:], 1.0))
    p.V(lambda e: e.memset(onesrow[:], 1.0))
    p.V(lambda e: e.memset(epscol[:], LN_EPS), inc=True)

    p.Gdma(rt_col[:], dt["taur"][:])
    dg_tau = p.c["DG"]
    p.Sc(lambda e: e.activation(out=rt_col[:], in_=rt_col[:], func=ACT.Abs),
         inc=True, waits=[("DG", dg_tau)])
    p.V(lambda e: e.tensor_scalar_add(rt_col[:], rt_col[:], 0.1),
        waits=[("S", p.c["S"])])
    p.V(lambda e: e.reciprocal(rt_col[:], rt_col[:]), inc=True)
    p.V(lambda e: e.tensor_copy(out=rt_row[0:1, 3:4], in_=rt_col[0:1, 0:1]), inc=True)
    v_rt = p.c["V"]

    for jc in range(NJ):
        p.T(lambda e, jc=jc: e.matmul(out=pv2[:, jc * 64:(jc + 1) * 64], lhsT=si0b[0:NSL, jc * 128:(jc + 1) * 128], rhs=identb[0:NSL, 0:NSL], start=True, stop=True),
            inc=(jc == NJ - 1), waits=[("DG", dg_si)] if jc == 0 else ())
    p.V(lambda e: e.tensor_copy(out=si0T[:], in_=pv2[:, 0:NJ * 64]), inc=True,
        waits=[("T", p.c["T"])])
    v_si0T = p.c["V"]

    sctx = ExitStack()
    whh_sb = sctx.enter_context(nc.sbuf_tensor("WhhT_sb", [128, NJ * 3 * D], BF16))
    bhh_bc = sctx.enter_context(nc.sbuf_tensor("bhh_bc", [64, 3 * D], BF16))
    gh_stage = sctx.enter_context(nc.sbuf_tensor("gh_stage", [64, 3 * D], F32))
    if True:
        p.Gdma(whh_sb[:], dt["WhhT"][:].rearrange("(c p) f -> p c f", p=128))
        dg_whh = p.c["DG"]
        p.Gdma(bhh_bc[:], dt["bhhc"][:])
        dg_bhh = p.c["DG"]
        evac_v = {0: 0, 1: 0, 2: 0, 3: 0}
        for rc in range(6):
            qtr = rc % 4
            w_ = [("V", evac_v[qtr])]
            if rc == 0:
                w_ += [("DG", dg_whh), ("V", v_si0T)]
            for dc in range(NJ):
                p.T(lambda e, rc=rc, dc=dc, q_=qtr: e.matmul(
                        out=psS[0:NSL, q_ * 512:(q_ + 1) * 512],
                        lhsT=si0T[:, dc * NSL:(dc + 1) * NSL],
                        rhs=whh_sb[:, dc * 3 * D + rc * 512: dc * 3 * D + (rc + 1) * 512],
                        start=(dc == 0), stop=(dc == NJ - 1)),
                    inc=(dc == NJ - 1), waits=w_ if dc == 0 else ())
            tmm = p.c["T"]
            p.V(lambda e, rc=rc, q_=qtr: e.tensor_add(
                    out=gh_stage[:, rc * 512:(rc + 1) * 512],
                    in0=psS[0:NSL, q_ * 512:(q_ + 1) * 512],
                    in1=bhh_bc[:, rc * 512:(rc + 1) * 512]),
                inc=True, waits=[("T", tmm), ("DG", dg_bhh)])
            evac_v[qtr] = p.c["V"]
        p.Gdma(gh_d[:], gh_stage[:], waits=[("V", p.c["V"])])
        dg_gh_store = p.c["DG"]
        if dbg:
            p.Gdma(taps["t_gh"][:], gh_stage[:], waits=[("V", p.c["V"])])
    sctx.close()
    t_after_setup = p.c["T"]

    actx = ExitStack()
    AE = actx.enter_context
    cosb = AE(nc.sbuf_tensor("cosb", [128, S], BF16))
    sinb = AE(nc.sbuf_tensor("sinb", [128, S], BF16))
    mextb = AE(nc.sbuf_tensor("mextb", [128, 896], BF16))
    xstage = AE(nc.sbuf_tensor("xstage", [128, D], F32))
    xnt = AE(nc.sbuf_tensor("xnt", [128, D], BF16))
    xnT = AE(nc.sbuf_tensor("xnT", [128, NJ * S], BF16))
    wslA = AE(nc.sbuf_tensor("wslotA", [128, 4096], BF16))
    wslB = AE(nc.sbuf_tensor("wslotB", [128, 4096], BF16))
    qT = AE(nc.sbuf_tensor("qT", [128, NJ * S], BF16))
    kT = AE(nc.sbuf_tensor("kT", [128, NJ * S], BF16))
    vpad = AE(nc.sbuf_tensor("vpad", [128, NT * H * 65], BF16))
    rtmp = AE(nc.sbuf_tensor("rtmp", [128, D], BF16))
    Psb = AE(nc.sbuf_tensor("Psb", [128, 3 * 1024], BF16))
    rl_sb = AE(nc.sbuf_tensor("rl_sb", [1, 2048], BF16))
    rl_bc = AE(nc.sbuf_tensor("rl_bc", [64, 2048], BF16))
    lnrow = AE(nc.sbuf_tensor("lnrow", [1, 1024], F32))
    attn_sb = AE(nc.sbuf_tensor("attn_sb", [128, NJ * QCW], BF16))
    o2_sb = AE(nc.sbuf_tensor("o2_sb", [128, D], F32))
    xstage2 = AE(nc.sbuf_tensor("xstage2", [128, D], F32))
    if True:

        p.Gdma(cosb[:], dt["cosc"][:], waits=[("T", t_after_setup)])
        p.Gdma(sinb[:], dt["sinc"][:])
        p.Gdma(mextb[:], dt["mextc"][:])
        dg_tabs = p.c["DG"]
        p.V(lambda e: e.memset(
            vpad[:].rearrange("p (th w) -> p th w", w=65)[:, :, 64:65], 1.0),
            inc=True)
        v_ones = p.c["V"]

        xstage_cons = [0, 0]
        x2_cons = [0, 0]
        for b in range(BL):
            tw = t_after_setup if b == 0 else p.c["T"]
            dgWq = [p.Gdma(wslA[:], dt["WqT"][:, 0:512].rearrange("(c p) f -> p c f", p=128),
                           waits=[("T", tw)]),
                    p.Gdma(wslB[:], dt["WqT"][:, 512:1024].rearrange("(c p) f -> p c f", p=128))]
            tr_done = p.c["T"]
            for tt in range(NT):
                xw = [("V", x2_cons[0])]
                if b == 0 and tt < 1:
                    xw += [("DG", dg_gh_store), ("T", t_after_setup)]
                dy = p.Ydma(xstage2[:, 0:D], x_d_slice(dt, b, tt), waits=xw)
                xs = xstage2[:, 0:D]
                idx = b * NT + tt
                p.V(lambda e, xs=xs: e.bn_stats(out=bnst[:, 0:6], in_=xs[:, 0:512]),
                    waits=[("DY", dy)])
                p.V(lambda e, xs=xs: e.bn_stats(out=bnst[:, 6:12], in_=xs[:, 512:1024]),
                    inc=True)
                vbs = p.c["V"]
                p.V(lambda e: e.bn_aggr(out=bnagg[:], in_=bnst[:]), inc=True,
                    waits=[("V", vbs)])
                vv = p.c["V"]
                p.V(lambda e, idx=idx: e.tensor_copy(out=sxbuf[:, idx:idx + 1],
                                                     in_=bnagg[:, 0:1]),
                    waits=[("V", vv)])
                p.Sc(lambda e: e.activation(out=cols[:, 4:5], in_=bnagg[:, 1:2],
                                            func=ACT.Ln, bias=epscol[:, 0:1], scale=1.0),
                     inc=True, waits=[("V", vv)])
                sl1 = p.c["S"]
                p.Sc(lambda e: e.activation(out=cols[:, 5:6], in_=cols[:, 4:5],
                                            func=ACT.Exp, scale=-0.5), inc=True,
                     waits=[("S", sl1)])
                ss = p.c["S"]
                if dbg and b == 0:
                    p.V(lambda e, tt=tt: e.tensor_copy(out=dbgst[:, tt * 4:tt * 4 + 4],
                                                       in_=cols[:, 0:4]),
                        waits=[("S", ss)])
                    p.V(lambda e, tt=tt: e.tensor_copy(out=dbgst[:, 63:64],
                                                       in_=cols[:, 5:6]))
                p.V(lambda e, xs=xs: e.tensor_scalar(
                        out=xnt[:], in0=xs, scalar1=bnagg[:, 0:1], scalar2=cols[:, 5:6],
                        op0=ALU.subtract, op1=ALU.mult),
                    inc=True, waits=[("S", ss), ("T", tr_done)])
                x2_cons[0] = p.c["V"]
                vxn = p.c["V"]
                for jc in range(NJ):
                    p.T(lambda e, jc=jc: e.matmul(out=pv2[:, jc * 128:(jc + 1) * 128], lhsT=xnt[:, jc * 128:(jc + 1) * 128], rhs=identb[:], start=True, stop=True),
                        inc=(jc == NJ - 1),
                        waits=[("V", vxn)] if jc == 0 else ())
                tposted = p.c["T"]
                p.V(lambda e, tt=tt: e.tensor_copy(
                        out=xnT[:].rearrange("p (c w) -> p c w", w=S)[:, :, tt * 128:(tt + 1) * 128],
                        in_=pv2[:, 0:1024]), inc=True, waits=[("T", tposted)])
                tr_done = p.c["T"]
                if tt == 7:
                    v_xnT_h0 = p.c["V"]
            v_xnT = p.c["V"]
            v_xnT_h = [v_xnT_h0, v_xnT]

            s0 = p.c["S"]
            evac_s = {0: s0, 1: s0, ("wo", 0): s0, ("wo", 1): s0}

            def proj_fm(dst, wdg, rope):
                for ic in range(NJ):
                    sl = wslA if ic < 4 else wslB
                    wcol = (ic % 4) * 128
                    for th in range(2):
                        half = (ic * 2 + th) % 2
                        w_ = [("DG", wdg[0 if ic < 4 else 1]), ("V", v_xnT_h[th]),
                              ("S", evac_s[half])]
                        for sub in range(2):
                            for j in range(NJ):
                                p.T(lambda e, j=j, sl=sl, wcol=wcol, th=th, sub=sub, half=half:
                                    e.matmul(
                                        out=psS[:, half * 1024 + sub * 512:
                                                half * 1024 + (sub + 1) * 512],
                                        lhsT=sl[:, j * 512 + wcol: j * 512 + wcol + 128],
                                        rhs=xnT[:, j * S + th * 1024 + sub * 512:
                                                j * S + th * 1024 + (sub + 1) * 512],
                                        start=(j == 0), stop=(j == NJ - 1)),
                                    inc=(sub == 1 and j == NJ - 1),
                                    waits=w_ if (sub == 0 and j == 0) else ())
                        tmm = p.c["T"]
                        dslice = dst[:, ic * S + th * 1024: ic * S + (th + 1) * 1024]
                        p.Sc(lambda e, dslice=dslice, half=half: e.copy(
                                out=dslice, in_=psS[:, half * 1024:(half + 1) * 1024]),
                             inc=True, waits=[("T", tmm)])
                        evac_s[half] = p.c["S"]
                        if rope:
                            scp = p.c["S"]
                            cs = cosb[:, th * 1024:(th + 1) * 1024]
                            sn = sinb[:, th * 1024:(th + 1) * 1024]
                            p.V(lambda e, dslice=dslice: e.stream_shuffle(
                                    out=rtmp[:], in_=dslice, mask=SHUF_MASK),
                                waits=[("S", scp), ("DG", dg_tabs)])
                            p.V(lambda e, sn=sn: e.tensor_mul(out=rtmp[:], in0=rtmp[:], in1=sn))
                            p.V(lambda e, dslice=dslice, cs=cs: e.tensor_mul(
                                    out=dslice, in0=dslice, in1=cs))
                            p.V(lambda e, dslice=dslice: e.tensor_add(
                                    out=dslice, in0=dslice, in1=rtmp[:]), inc=True)

            if dbg and b == 0:
                p.Gdma(taps["t_xnT"][:], xnT[:], waits=[("V", v_xnT)])
                p.Gdma(taps["t_sx"][:], sxbuf[:], waits=[("V", v_xnT)])
                p.Gdma(taps["t_dbgst"][:], dbgst[:], waits=[("V", v_xnT)])
            proj_fm(qT, dgWq, rope=True)
            t_q = p.c["T"]
            dgWk = [p.Gdma(wslA[:], dt["WkT"][:, 0:512].rearrange("(c p) f -> p c f", p=128),
                           waits=[("T", t_q)]),
                    p.Gdma(wslB[:], dt["WkT"][:, 512:1024].rearrange("(c p) f -> p c f", p=128))]
            proj_fm(kT, dgWk, rope=True)
            t_k = p.c["T"]
            dgWv = [p.Gdma(wslA[:], dt["WvT"][:, 0:512].rearrange("(c p) f -> p c f", p=128),
                           waits=[("T", t_k)]),
                    p.Gdma(wslB[:], dt["WvT"][:, 512:1024].rearrange("(c p) f -> p c f", p=128))]
            for tt in range(NT):
                half = tt % 2
                w_ = [("DG", dgWv[1]), ("V", v_ones), ("S", evac_s[half])]
                for hf in range(2):
                    sl = wslA if hf == 0 else wslB
                    for j in range(NJ):
                        p.T(lambda e, tt=tt, hf=hf, j=j, sl=sl, half=half: e.matmul(
                                out=psS[:, half * 1024 + hf * 512: half * 1024 + hf * 512 + 512],
                                lhsT=xnT[:, j * S + tt * 128: j * S + tt * 128 + 128],
                                rhs=sl[:, j * 512:(j + 1) * 512],
                                start=(j == 0), stop=(j == NJ - 1)),
                            inc=(hf == 1 and j == NJ - 1),
                            waits=w_ if (hf == 0 and j == 0) else ())
                tmm = p.c["T"]
                p.Sc(lambda e, tt=tt, half=half: e.copy(
                        out=vpad[:, tt * H * 65:(tt + 1) * H * 65]
                            .rearrange("p (h w) -> p h w", w=65)[:, :, 0:64],
                        in_=psS[:, half * 1024:(half + 1) * 1024]
                            .rearrange("p (h w) -> p h w", w=64)),
                     inc=True, waits=[("T", tmm)])
                evac_s[half] = p.c["S"]
            t_v = p.c["T"]
            s_v = p.c["S"]
            dgWo = [p.Gdma(wslA[:], dt["WoT"][:, 0:512].rearrange("(c p) f -> p c f", p=128),
                           waits=[("T", t_v)]),
                    p.Gdma(wslB[:], dt["WoT"][:, 512:1024].rearrange("(c p) f -> p c f", p=128))]

            pv_free = {0: p.c["V"], 1: p.c["V"]}
            attn_free_t = p.c["T"]
            for qc in range(NQC):
                nkt = 4 * (qc + 1)
                def emit_tail(ctx):
                    pvc, rlslot, t_pv, hp = ctx
                    rlr = rl_sb[0:1, (hp % 2) * 1024:(hp % 2) * 1024 + 1024]
                    p.Sc(lambda e, pvc=pvc: e.activation(
                            out=lnrow[0:1, :], in_=pvc[64:65, 0:1024], func=ACT.Ln),
                         inc=True, waits=[("T", t_pv)])
                    s_ln = p.c["S"]
                    p.Sc(lambda e, pvc=pvc, rlr=rlr: e.activation(out=rlr, in_=lnrow[0:1, :],
                                                func=ACT.Exp, scale=-1.0),
                         inc=True, waits=[("S", s_ln)])
                    s_rl = p.c["S"]
                    for hb in range(2):
                        p.T(lambda e, hb=hb, pvc=pvc, rlr=rlr: e.matmul(
                                out=pvc[64:128, hb * 512:(hb + 1) * 512],
                                lhsT=onesrow[0:1, :],
                                rhs=rlr[0:1, hb * 512:(hb + 1) * 512],
                                start=True, stop=True),
                            inc=(hb == 1), waits=[("S", s_rl)] if hb == 0 else ())
                    t_bc = p.c["T"]
                    p.V(lambda e, pvc=pvc, rlslot=rlslot: e.tensor_copy(
                            out=rlslot, in_=pvc[64:128, 0:1024]),
                        inc=True, waits=[("T", t_bc)])
                    v_rlbc = p.c["V"]
                    p.V(lambda e, hp=hp, pvc=pvc, rlslot=rlslot: e.tensor_mul(
                            out=attn_sb[0:64, hp * QCW:(hp + 1) * QCW],
                            in0=pvc[0:64, 0:512], in1=rlslot[:, 0:512]),
                        waits=[("V", v_rlbc), ("T", attn_free_t)])
                    p.V(lambda e, hp=hp, pvc=pvc, rlslot=rlslot: e.tensor_mul(
                            out=attn_sb[64:128, hp * QCW:(hp + 1) * QCW],
                            in0=pvc[0:64, 512:1024], in1=rlslot[:, 512:1024]),
                        inc=True)
                    pv_free[hp % 2] = p.c["V"]

                prev_ctx = None
                for hp in range(NJ):
                    pvc = pv if hp % 2 == 0 else pv2
                    rlslot = rl_bc[:, (hp % 2) * 1024:(hp % 2) * 1024 + 1024]
                    gm = {}
                    exp_done = {}

                    def emit_pv(kt):
                        pslot = Psb[:, (kt % 3) * 1024:(kt % 3) * 1024 + 1024]
                        for hh in range(2):
                            w2_ = [("S", exp_done[kt])] + gm[kt]
                            if kt == 0 and hh == 0:
                                w2_.append(("V", pv_free[hp % 2]))
                            p.T(lambda e, hp=hp, kt=kt, hh=hh, pslot=pslot, nkt=nkt,
                                        pvc=pvc: e.matmul(
                                    out=pvc[0:65, hh * 512:(hh + 1) * 512],
                                    lhsT=vpad[:, kt * H * 65 + (2 * hp + hh) * 65:
                                              kt * H * 65 + (2 * hp + hh) * 65 + 65],
                                    rhs=pslot[:, hh * 512:(hh + 1) * 512],
                                    start=(kt == 0), stop=(kt == nkt - 1)),
                                inc=(hh == 1), waits=w2_ if hh == 0 else ())

                    for kt in range(nkt):
                        par = kt % 2
                        w_ = [("S", evac_s[par])]
                        if kt == 0:
                            w_.append(("V", pv_free[hp % 2]))
                        for hh in range(2):
                            p.T(lambda e, hp=hp, qc=qc, kt=kt, hh=hh, par=par: e.matmul(
                                    out=psS[:, par * 1024 + hh * 512: par * 1024 + hh * 512 + 512],
                                    lhsT=kT[hh * 64:(hh + 1) * 64,
                                            hp * S + kt * 128: hp * S + (kt + 1) * 128],
                                    rhs=qT[hh * 64:(hh + 1) * 64,
                                           hp * S + qc * QCW: hp * S + (qc + 1) * QCW],
                                    start=True, stop=True,
                                    tile_position=(hh * 64, 0)),
                                inc=(hh == 1), waits=w_ if hh == 0 else ())
                        tsc = p.c["T"]
                        pslot = Psb[:, (kt % 3) * 1024:(kt % 3) * 1024 + 1024]
                        p.Sc(lambda e, pslot=pslot, par=par: e.activation(
                                out=pslot, in_=psS[:, par * 1024:(par + 1) * 1024],
                                func=ACT.Exp, scale=SCALE),
                             inc=True, waits=[("T", tsc)])
                        evac_s[par] = p.c["S"]
                        exp_done[kt] = p.c["S"]
                        jdiag = kt - 4 * qc
                        if 0 <= jdiag <= 3:
                            st = 384 - 128 * jdiag
                            mvA = mextb[:, st:st + 512]
                            p.V(lambda e, pslot=pslot, mvA=mvA: e.tensor_mul(
                                    out=pslot[:, 0:512], in0=pslot[:, 0:512], in1=mvA),
                                inc=True, waits=[("S", exp_done[kt])])
                            p.G(lambda e, pslot=pslot, mvA=mvA: e.tensor_mul(
                                    out=pslot[:, 512:1024], in0=pslot[:, 512:1024], in1=mvA),
                                inc=True, waits=[("S", exp_done[kt])])
                            gm[kt] = [("V", p.c["V"]), ("G", p.c["G"])]
                        else:
                            gm[kt] = []
                        if kt > 1:
                            emit_pv(kt - 2)
                    if nkt > 1:
                        emit_pv(nkt - 2)
                    emit_pv(nkt - 1)
                    t_pv = p.c["T"]
                    if prev_ctx is not None:
                        emit_tail(prev_ctx)
                    prev_ctx = (pvc, rlslot, t_pv, hp)
                emit_tail(prev_ctx)
                prev_ctx = None
                if dbg and qc == 0 and b == 0:
                    p.Gdma(taps["t_attn"][:], attn_sb[:], waits=[("V", p.c["V"])])
                v_attn = p.c["V"]
                for ts in range(4):
                    half = ts % 2
                    dyx = p.Ydma(xstage[:, 0:D],
                                 x_d_slice(dt, b, qc * 4 + ts),
                                 waits=[("V", xstage_cons[0])])
                    w_ = [("V", v_attn), ("DG", dgWo[1]),
                          ("V", p.c["V"]), ("S", evac_s[half])]
                    for hf in range(2):
                        sl = wslA if hf == 0 else wslB
                        for dc in range(NJ):
                            p.T(lambda e, ts=ts, hf=hf, dc=dc, sl=sl, half=half: e.matmul(
                                    out=psS[:, half * 1024 + hf * 512:
                                            half * 1024 + hf * 512 + 512],
                                    lhsT=attn_sb[:, dc * QCW + ts * 128:
                                                 dc * QCW + ts * 128 + 128],
                                    rhs=sl[:, dc * 512:(dc + 1) * 512],
                                    start=(dc == 0), stop=(dc == NJ - 1)),
                                inc=(hf == 1 and dc == NJ - 1),
                                waits=w_ if (hf == 0 and dc == 0) else ())
                    t_wo = p.c["T"]
                    p.V(lambda e, half=half: e.tensor_copy(
                            out=o2_sb[:], in_=psS[:, half * 1024:(half + 1) * 1024]),
                        inc=True, waits=[("T", t_wo)])
                    v_ev = p.c["V"]
                    p.Sc(lambda e: e.engine_nop() if False else e.copy(
                            out=cols[:, 12:13], in_=o2_sb[:, 0:1]),
                         inc=True, waits=[("V", v_ev)])
                    s_ev = p.c["S"]
                    evac_s[half] = s_ev
                    p.V(lambda e: e.bn_stats(out=bnst[:, 0:6], in_=o2_sb[:, 0:512]))
                    p.V(lambda e: e.bn_stats(out=bnst[:, 6:12], in_=o2_sb[:, 512:1024]),
                        inc=True)
                    vbs = p.c["V"]
                    p.V(lambda e: e.bn_aggr(out=bnagg[:], in_=bnst[:]), inc=True,
                        waits=[("V", vbs)])
                    vst = p.c["V"]
                    p.Sc(lambda e: e.activation(out=cols[:, 4:5], in_=bnagg[:, 1:2],
                                                func=ACT.Ln, bias=epscol[:, 0:1], scale=1.0),
                         inc=True, waits=[("V", vst)])
                    sl1 = p.c["S"]
                    p.Sc(lambda e: e.activation(out=cols[:, 5:6], in_=cols[:, 4:5],
                                                func=ACT.Exp, scale=-0.5), inc=True,
                         waits=[("S", sl1)])
                    s_rs = p.c["S"]
                    p.V(lambda e: e.tensor_scalar(
                            out=o2_sb[:], in0=o2_sb[:], scalar1=bnagg[:, 0:1],
                            scalar2=cols[:, 5:6], op0=ALU.subtract, op1=ALU.mult),
                        waits=[("S", s_rs), ("DY", dyx)])
                    p.V(lambda e: e.tensor_add(
                            out=xstage[:, 0:D], in0=o2_sb[:],
                            in1=xstage[:, 0:D]), inc=True)
                    vy = p.c["V"]
                    p.Ydma(y_d[b * S + qc * QCW + ts * 128:
                               b * S + qc * QCW + ts * 128 + 128, :],
                           xstage[:, 0:D], waits=[("V", vy)])
                    xstage_cons[0] = vy
                attn_free_t = p.c["T"]
            if dbg and b == 0:
                p.Gdma(taps["t_qT"][:], qT[:], waits=[("V", p.c["V"])])
                p.Gdma(taps["t_kT"][:], kT[:], waits=[("V", p.c["V"])])
                p.Gdma(taps["t_vp"][:], vpad[:], waits=[("S", p.c["S"])])

    actx.close()
    barT, barV, barS, barG = p.c["T"], p.c["V"], p.c["S"], p.c["G"]
    barDY = p.c["DY"]
    bar = [("T", barT), ("V", barV), ("S", barS), ("G", barG)]

    fctx = ExitStack()
    FE = fctx.enter_context
    ystage = FE(nc.sbuf_tensor("ystage", [128, 2 * D], F32))
    big48 = FE(nc.sbuf_tensor("big48", [128, 24576], BF16))
    big32 = FE(nc.sbuf_tensor("big32", [128, 16384], BF16))
    w1wo = FE(nc.sbuf_tensor("w1wo", [128, 8192], BF16))
    otbuf = FE(nc.sbuf_tensor("otbuf", [128, 4096], BF16))
    alpha_bf = FE(nc.sbuf_tensor("alpha_bf", [128, NT * NSL], BF16))
    alphaT = FE(nc.sbuf_tensor("alphaT", [64, S], BF16))
    fsm = FE(nc.sbuf_tensor("fsmall", [128, 256], F32))
    maskb = FE(nc.sbuf_tensor("maskb", [128, NSL], BF16))
    gh_bf = FE(nc.sbuf_tensor("gh_bf", [64, 3 * D], BF16))
    bih_bc = FE(nc.sbuf_tensor("bih_bc", [64, 3 * D], BF16))
    si_bf = FE(nc.sbuf_tensor("si_bf", [64, D], BF16))
    siT = FE(nc.sbuf_tensor("siT", [128, NJ * NSL], BF16))
    gtmp = FE(nc.sbuf_tensor("gtmp", [64, D], F32))
    gtmp2 = FE(nc.sbuf_tensor("gtmp2", [64, D], F32))
    r_sb = FE(nc.sbuf_tensor("r_sb", [64, D], F32))
    z_sb = FE(nc.sbuf_tensor("z_sb", [64, D], F32))
    snew_f = FE(nc.sbuf_tensor("snew_f", [64, D], F32))
    snew_bf = FE(nc.sbuf_tensor("snew_bf", [64, D], BF16))
    snewT = FE(nc.sbuf_tensor("snewT", [128, NJ * NSL], BF16))
    q2_bf = FE(nc.sbuf_tensor("q2_bf", [64, DQ], BF16))
    k2_bf = FE(nc.sbuf_tensor("k2_bf", [64, DQ], BF16))
    qT2 = FE(nc.sbuf_tensor("qT2", [128, 2 * NSL], BF16))
    kT2 = FE(nc.sbuf_tensor("kT2", [128, 2 * NSL], BF16))
    vs_sb = FE(nc.sbuf_tensor("vs_sb", [64, D], BF16))
    e2_sb = FE(nc.sbuf_tensor("e2_sb", [64, NSL], F32))
    aP_bf = FE(nc.sbuf_tensor("aP_bf", [64, NSL], BF16))
    aPT = FE(nc.sbuf_tensor("aPT", [64, NSL], BF16))
    y2n_bf = FE(nc.sbuf_tensor("y2n_bf", [64, D], BF16))
    y2nT = FE(nc.sbuf_tensor("y2nT", [128, NJ * NSL], BF16))
    sv_bf = FE(nc.sbuf_tensor("sv_bf", [64, D], BF16))
    fo_sb = FE(nc.sbuf_tensor("fo_sb", [128, D], F32))
    fscr = FE(nc.sbuf_tensor("fscr", [128, D], BF16))
    routes = FE(nc.sbuf_tensor("routes", [128, NT * NSL], F32))
    m8b = FE(nc.sbuf_tensor("m8b", [128, NT * 8], F32))
    bo_bc = FE(nc.sbuf_tensor("bo_bc", [128, D], BF16))
    b2rt_bc = FE(nc.sbuf_tensor("b2rt_bc", [128, NSL], F32))
    lng_bc = FE(nc.sbuf_tensor("lng_bc", [64, D], BF16))
    lnb_bc = FE(nc.sbuf_tensor("lnb_bc", [64, D], BF16))
    bq_bc = FE(nc.sbuf_tensor("bq_bc", [64, DQ], BF16))
    bk_bc = FE(nc.sbuf_tensor("bk_bc", [64, DQ], BF16))
    bv_bc = FE(nc.sbuf_tensor("bv_bc", [64, D], BF16))
    bvp_bc = FE(nc.sbuf_tensor("bvp_bc", [64, D], BF16))
    b1c1_rowb = FE(nc.sbuf_tensor("b1c1_rowb", [1, DH], BF16))
    wq_sb = FE(nc.sbuf_tensor("wq_sb", [128, NJ * DQ], BF16))
    wk_sb = FE(nc.sbuf_tensor("wk_sb", [128, NJ * DQ], BF16))
    w2_sb = FE(nc.sbuf_tensor("w2_sb", [128, 4 * NSL], BF16))
    ynT = big48[:, 0:16384]
    hT = big48[:, 16384:24576]
    yn_sb = big32[:, 0:16384]
    y2_f = gtmp
    wv_sb = big32[:, 0:8192]
    wvp_sb = big32[:, 8192:16384]

    p.Gdma(gh_bf[:], gh_d[:], waits=bar + [("DG", dg_gh_store)])
    dg_ghload = p.c["DG"]
    p.Gdma(bih_bc[:], dt["bihc"][:])
    g_bih = p.c["G"]
    p.Gdma(bo_bc[:], dt["boc"][:])
    p.Gdma(lng_bc[:], dt["lngc"][:])
    p.Gdma(lnb_bc[:], dt["lnbc"][:])
    p.Gdma(bq_bc[:], dt["bqc"][:])
    p.Gdma(bk_bc[:], dt["bkc"][:])
    p.Gdma(bv_bc[:], dt["bvc"][:])
    p.Gdma(bvp_bc[:], dt["bvpc"][:])
    p.Gdma(b2rt_bc[:], dt["b2bc"][:])
    dg_b2 = p.c["DG"]
    p.V(lambda e: e.tensor_scalar(out=b2rt_bc[:], in0=b2rt_bc[:],
                                  scalar1=rt_col[:], scalar2=None, op0=ALU.mult),
        inc=True, waits=[("DG", dg_b2), ("V", v_rt), ("V", barV)])
    v_b2rt = p.c["V"]
    g_after_bcasts = p.c["G"]

    dgw1 = p.Gdma(w1wo[:, 0:4096], dt["w1"][0:1024, :].rearrange("(c p) f -> p c f", p=128),
                  waits=bar)
    dgw2 = p.Gdma(w2_sb[:], dt["w2"][:].rearrange("(c p) f -> p c f", p=128))
    dgwq = p.Gdma(wq_sb[:], dt["wq"][:].rearrange("(c p) f -> p c f", p=128))
    dgwk = p.Gdma(wk_sb[:], dt["wk"][:].rearrange("(c p) f -> p c f", p=128))

    p.Gdma(otbuf[:], dt["w1"][1024:2048, :].rearrange("(c p) f -> p c f", p=128),
           waits=bar)
    dgw1b = p.c["DG"]
    p.T(lambda e: e.matmul(out=pv[0:1, 0:512], lhsT=onescol[0:64, 0:1],
                           rhs=si0b[:, 0:512], start=True, stop=True), waits=bar)
    p.T(lambda e: e.matmul(out=pv[0:1, 512:1024], lhsT=onescol[0:64, 0:1],
                           rhs=si0b[:, 512:1024], start=True, stop=True), inc=True)
    tsm = p.c["T"]
    p.V(lambda e: e.tensor_scalar(out=rowstage[0:1, 0:D], in0=pv[0:1, 0:1024],
                                  scalar1=1.0 / NSL, scalar2=None, op0=ALU.mult),
        inc=True, waits=[("T", tsm), ("G", g_bih)])
    vsm = p.c["V"]
    for jc in range(NJ):
        p.T(lambda e, jc=jc: e.matmul(
                out=pv[0:128, jc:jc + 1], lhsT=rowstage[0:1, jc * 128:(jc + 1) * 128],
                rhs=onescol[0:1, 0:1], start=True, stop=True),
            inc=(jc == NJ - 1), waits=[("V", vsm)] if jc == 0 else ())
    tsmT = p.c["T"]
    p.V(lambda e: e.tensor_copy(out=siT[:, 0:NJ], in_=pv[0:128, 0:NJ]), inc=True,
        waits=[("T", tsmT)])
    vsmT = p.c["V"]
    for j in range(NJ):
        p.T(lambda e, j=j: e.matmul(
                out=pv[0:1, 0:512], lhsT=siT[:, j:j + 1],
                rhs=otbuf[:, j * 512:(j + 1) * 512],
                start=(j == 0), stop=(j == NJ - 1)),
            inc=(j == NJ - 1),
            waits=[("V", vsmT), ("DG", dgw1b)] if j == 0 else ())
    tc1 = p.c["T"]
    p.Gdma(rowf[0:1, 0:DH], dt["b1r"][:], waits=[("G", p.c["G"])])
    dgb1 = p.c["DG"]
    p.V(lambda e: e.tensor_add(out=rowf[0:1, 0:DH], in0=pv[0:1, 0:512],
                               in1=rowf[0:1, 0:DH]),
        waits=[("T", tc1), ("DG", dgb1)])
    p.V(lambda e: e.tensor_copy(out=b1c1_rowb[0:1, :], in_=rowf[0:1, 0:DH]), inc=True)
    vb1 = p.c["V"]
    for j in range(4):
        p.T(lambda e, j=j: e.matmul(
                out=pv[0:128, NJ + j:NJ + j + 1], lhsT=b1c1_rowb[0:1, j * 128:(j + 1) * 128],
                rhs=onescol[0:1, 0:1], start=True, stop=True),
            inc=(j == 3), waits=[("V", vb1)] if j == 0 else ())
    tb1 = p.c["T"]
    p.V(lambda e: e.tensor_copy(out=b1c1_col[:], in_=pv[0:128, NJ:NJ + 4]), inc=True,
        waits=[("T", tb1)])
    v_b1c1 = p.c["V"]

    ystage_cons = [0, 0]
    for b in range(BL):
        if b == 1:
            dgw1 = p.Gdma(w1wo[:, 0:4096], dt["w1"][0:1024, :].rearrange("(c p) f -> p c f", p=128),
                          waits=[("T", p.c["T"])])
        tr_done = p.c["T"]
        guardT = barT if b == 0 else p.c["T"]
        guardS = barS if b == 0 else p.c["S"]
        for tt in range(NT):
            sl = tt % 2
            dy = p.Ydma(ystage[:, sl * D:(sl + 1) * D],
                        y_d[b * S + tt * 128: b * S + (tt + 1) * 128, :],
                        waits=[("V", ystage_cons[sl]), ("DY", barDY)])
            ys = ystage[:, sl * D:(sl + 1) * D]
            idx = b * NT + tt
            p.V(lambda e, ys=ys: e.bn_stats(out=bnst[:, 0:6], in_=ys[:, 0:512]),
                waits=([("DY", dy)] + ([("T", guardT), ("S", guardS)] if tt == 0 else [])))
            p.V(lambda e, ys=ys: e.bn_stats(out=bnst[:, 6:12], in_=ys[:, 512:1024]),
                inc=True)
            vbs = p.c["V"]
            p.V(lambda e: e.bn_aggr(out=bnagg[:], in_=bnst[:]), inc=True,
                waits=[("V", vbs)])
            vv = p.c["V"]
            p.Sc(lambda e: e.activation(out=cols[:, 4:5], in_=bnagg[:, 1:2],
                                        func=ACT.Ln, bias=epscol[:, 0:1], scale=1.0),
                 inc=True, waits=[("V", vv)])
            sl1 = p.c["S"]
            p.Sc(lambda e: e.activation(out=cols[:, 5:6], in_=cols[:, 4:5],
                                        func=ACT.Exp, scale=-0.5), inc=True,
                 waits=[("S", sl1)])
            ss = p.c["S"]
            p.V(lambda e, ys=ys, tt=tt: e.tensor_scalar(
                    out=yn_sb[:, tt * D:(tt + 1) * D], in0=ys,
                    scalar1=bnagg[:, 0:1], scalar2=cols[:, 5:6],
                    op0=ALU.subtract, op1=ALU.mult),
                inc=True, waits=[("S", ss)])
            ystage_cons[sl] = p.c["V"]
            vyn = p.c["V"]
            for jc in range(NJ):
                p.T(lambda e, tt=tt, jc=jc: e.matmul(out=pv2[:, jc * 128:(jc + 1) * 128], lhsT=yn_sb[:, tt * D + jc * 128: tt * D + (jc + 1) * 128], rhs=identb[:], start=True, stop=True),
                    inc=(jc == NJ - 1),
                    waits=[("V", vyn), ("T", tr_done)] if jc == 0 else ())
            tp = p.c["T"]
            p.V(lambda e, tt=tt: e.tensor_copy(
                    out=ynT.rearrange("p (c w) -> p c w", w=S)[:, :, tt * 128:(tt + 1) * 128],
                    in_=pv2[:, 0:1024]), inc=True, waits=[("T", tp)])
            tr_done = p.c["T"]
        v_ynT = p.c["V"]
        if dbg and b == 0:
            p.Gdma(taps["t_yn"][:], yn_sb, waits=[("V", v_ynT)])

        evacB1 = {i: p.c["S"] for i in range(4)}
        for uc in range(4):
            for tc in range(4):
                qtr = (uc * 4 + tc) % 4
                w_ = [("V", v_ynT), ("V", v_b1c1), ("S", evacB1[qtr]), ("DG", dgw1)]
                for j in range(NJ):
                    p.T(lambda e, uc=uc, tc=tc, j=j, q_=qtr: e.matmul(
                            out=psS[:, q_ * 512:(q_ + 1) * 512],
                            lhsT=w1wo[:, j * 512 + uc * 128: j * 512 + uc * 128 + 128],
                            rhs=ynT[:, j * S + tc * 512: j * S + (tc + 1) * 512],
                            start=(j == 0), stop=(j == NJ - 1)),
                        inc=(j == NJ - 1), waits=w_ if j == 0 else ())
                tmm = p.c["T"]
                p.Sc(lambda e, uc=uc, tc=tc, q_=qtr: e.activation(
                        out=hT[:, uc * S + tc * 512: uc * S + (tc + 1) * 512],
                        in_=psS[:, q_ * 512:(q_ + 1) * 512], func=ACT.Gelu,
                        bias=b1c1_col[:, uc:uc + 1], scale=1.0),
                     inc=True, waits=[("T", tmm)])
                evacB1[qtr] = p.c["S"]
        s_hT = p.c["S"]
        t_B1 = p.c["T"]
        if dbg and b == 0:
            p.Gdma(taps["t_hT"][:], hT, waits=[("S", s_hT)])
        dgwo = p.Gdma(w1wo[:], dt["wo"][:].rearrange("(c p) f -> p c f", p=128),
                      waits=[("T", t_B1)])
        wo_sb = w1wo

        evr = {0: p.c["V"], 1: p.c["V"]}
        for tt in range(NT):
            par = tt % 2
            w_ = [("S", s_hT), ("DG", dgw2), ("V", evr[par])]
            if tt == 0:
                w_.append(("V", v_b2rt))
            for uc in range(4):
                p.T(lambda e, tt=tt, uc=uc, par=par: e.matmul(
                        out=pv[0:128, par * 512: par * 512 + NSL],
                        lhsT=hT[:, uc * S + tt * 128: uc * S + (tt + 1) * 128],
                        rhs=w2_sb[:, uc * NSL:(uc + 1) * NSL],
                        start=(uc == 0), stop=(uc == 3)),
                    inc=(uc == 3), waits=w_ if uc == 0 else ())
            trt = p.c["T"]
            p.V(lambda e, tt=tt, par=par: e.scalar_tensor_tensor(
                    out=routes[:, tt * NSL:(tt + 1) * NSL],
                    in0=pv[0:128, par * 512: par * 512 + NSL], scalar=rt_col[:],
                    in1=b2rt_bc[:], op0=ALU.mult, op1=ALU.add),
                inc=True, waits=[("T", trt)])
            evr[par] = p.c["V"]
        for tt in range(NT):
            p.V(lambda e, tt=tt: e.max(out=m8b[:, tt * 8:(tt + 1) * 8],
                                       in_=routes[:, tt * NSL:(tt + 1) * NSL]),
                inc=(tt == NT - 1))
        vm8 = p.c["V"]
        for tt in range(NT):
            p.V(lambda e, tt=tt: e.tensor_scalar(
                    out=cols[:, 8:9], in0=m8b[:, tt * 8:tt * 8 + 1],
                    scalar1=-1.0, scalar2=None, op0=ALU.mult),
                inc=True, waits=[("V", vm8)])
            vneg = p.c["V"]
            p.Sc(lambda e, tt=tt: e.activation(
                    out=fsm[:, 128:192], in_=routes[:, tt * NSL:(tt + 1) * NSL],
                    func=ACT.Exp, bias=cols[:, 8:9], scale=1.0),
                 inc=True, waits=[("V", vneg)])
            sexp = p.c["S"]
            p.V(lambda e, tt=tt: e.tensor_single_scalar(
                    out=maskb[:], in_=routes[:, tt * NSL:(tt + 1) * NSL],
                    scalar=m8b[:, tt * 8 + 7:tt * 8 + 8], op=ALU.is_ge),
                waits=[("S", sexp)])
            p.V(lambda e: e.scalar_tensor_tensor(
                    out=fsm[:, 192:256], in0=fsm[:, 128:192], scalar=1.0, in1=maskb[:],
                    op0=ALU.bypass, op1=ALU.mult, accum_out=cols[:, 9:10]), inc=True)
            vem = p.c["V"]
            p.Sc(lambda e: e.activation(out=cols[:, 15:16], in_=cols[:, 9:10],
                                        func=ACT.Ln), inc=True, waits=[("V", vem)])
            srr0 = p.c["S"]
            p.Sc(lambda e: e.activation(out=cols[:, 10:11], in_=cols[:, 15:16],
                                        func=ACT.Exp, scale=-1.0), inc=True,
                 waits=[("S", srr0)])
            srr = p.c["S"]
            p.V(lambda e, tt=tt: e.tensor_scalar(
                    out=alpha_bf[:, tt * NSL:(tt + 1) * NSL], in0=fsm[:, 192:256],
                    scalar1=cols[:, 10:11], scalar2=None, op0=ALU.mult),
                inc=True, waits=[("S", srr)])
            va = p.c["V"]
            p.T(lambda e, tt=tt: e.matmul(out=pv2[0:64, (tt % 8) * 128:(tt % 8) * 128 + 128], lhsT=alpha_bf[:, tt * NSL:(tt + 1) * NSL], rhs=identb[:], start=True, stop=True),
                inc=True, waits=[("V", va)])
            ta = p.c["T"]
            p.V(lambda e, tt=tt: e.tensor_copy(
                    out=alphaT[0:64, tt * 128:(tt + 1) * 128],
                    in_=pv2[0:64, (tt % 8) * 128:(tt % 8) * 128 + 128]),
                inc=True, waits=[("T", ta)])
        v_alpha = p.c["V"]
        t_B2 = p.c["T"]
        if dbg and b == 0:
            p.Gdma(taps["t_alpha"][:], alpha_bf[:], waits=[("V", v_alpha)])
        dgwih = p.Gdma(big48[:], dt["WihT"][:].rearrange("(c p) f -> p c f", p=128),
                       waits=[("T", t_B2)])
        wih_sb = big48

        for tt in range(NT):
            w_ = [("V", v_alpha), ("S", p.c["S"]), ("V", p.c["V"])]
            p.T(lambda e, tt=tt: e.matmul(
                    out=psS[0:NSL, 0:512],
                    lhsT=alpha_bf[:, tt * NSL:(tt + 1) * NSL],
                    rhs=yn_sb[:, tt * D: tt * D + 512],
                    start=(tt == 0), stop=(tt == NT - 1)),
                waits=w_ if tt == 0 else ())
            p.T(lambda e, tt=tt: e.matmul(
                    out=psS[0:NSL, 512:1024],
                    lhsT=alpha_bf[:, tt * NSL:(tt + 1) * NSL],
                    rhs=yn_sb[:, tt * D + 512: tt * D + 1024],
                    start=(tt == 0), stop=(tt == NT - 1)))
            p.T(lambda e, tt=tt: e.matmul(
                    out=psS[0:NSL, 1024:1025],
                    lhsT=alpha_bf[:, tt * NSL:(tt + 1) * NSL],
                    rhs=onescol[:],
                    start=(tt == 0), stop=(tt == NT - 1)),
                inc=(tt == NT - 1))
        t_si = p.c["T"]
        p.V(lambda e: e.tensor_scalar_add(cols[0:NSL, 11:12],
                                          psS[0:NSL, 1024:1025], 1e-8),
            inc=True, waits=[("T", t_si)])
        vdn0 = p.c["V"]
        p.Sc(lambda e: e.activation(out=cols[0:NSL, 13:14], in_=cols[0:NSL, 11:12],
                                    func=ACT.Ln), inc=True, waits=[("V", vdn0)])
        sdn0 = p.c["S"]
        p.Sc(lambda e: e.activation(out=cols[0:NSL, 12:13], in_=cols[0:NSL, 13:14],
                                    func=ACT.Exp, scale=-1.0), inc=True,
             waits=[("S", sdn0)])
        sdn = p.c["S"]
        p.V(lambda e: e.tensor_scalar(
                out=si_bf[:], in0=psS[0:NSL, 0:1024], scalar1=cols[0:NSL, 12:13],
                scalar2=None, op0=ALU.mult), inc=True, waits=[("S", sdn)])
        v_si = p.c["V"]
        if dbg and b == 0:
            p.Gdma(taps["t_si"][:], si_bf[:], waits=[("V", v_si)])
        for jc in range(NJ):
            p.T(lambda e, jc=jc: e.matmul(out=pv2[:, jc * 64:(jc + 1) * 64], lhsT=si_bf[:, jc * 128:(jc + 1) * 128], rhs=identb[0:NSL, 0:NSL], start=True, stop=True),
                inc=(jc == NJ - 1), waits=[("V", v_si)] if jc == 0 else ())
        tsiT = p.c["T"]
        p.V(lambda e: e.tensor_copy(out=siT[:], in_=pv2[:, 0:NJ * 64]),
            inc=True, waits=[("T", tsiT)])
        v_siT = p.c["V"]
        dgwv = p.Gdma(wv_sb, dt["wv"][:].rearrange("(c p) f -> p c f", p=128)
                      .rearrange("p c f -> p (c f)") if False else
                      dt["wv"][:].rearrange("(c p) f -> p c f", p=128),
                      waits=[("T", t_si)])
        dgwvp = p.Gdma(wvp_sb, dt["wvp"][:].rearrange("(c p) f -> p c f", p=128))

        def gate_mm(g, w_extra):
            for hf in range(2):
                for dc in range(NJ):
                    p.T(lambda e, g=g, hf=hf, dc=dc: e.matmul(
                            out=psS[0:NSL, hf * 512:(hf + 1) * 512],
                            lhsT=siT[:, dc * NSL:(dc + 1) * NSL],
                            rhs=wih_sb[:, dc * 3 * D + g * D + hf * 512:
                                       dc * 3 * D + g * D + hf * 512 + 512],
                            start=(dc == 0), stop=(dc == NJ - 1)),
                        inc=(hf == 1 and dc == NJ - 1),
                        waits=w_extra if (hf == 0 and dc == 0) else ())
        gate_mm(0, [("V", v_siT), ("DG", dgwih)])
        t_g = p.c["T"]
        p.V(lambda e: e.tensor_add(out=gtmp[:], in0=psS[0:NSL, 0:1024],
                                   in1=gh_bf[:, 0:D]),
            waits=[("T", t_g), ("DG", dg_ghload), ("DG", g_bih)])
        p.V(lambda e: e.tensor_add(out=gtmp[:], in0=gtmp[:], in1=bih_bc[:, 0:D]),
            inc=True)
        vg = p.c["V"]
        p.Sc(lambda e: e.activation(out=r_sb[:], in_=gtmp[:], func=ACT.Sigmoid),
             inc=True, waits=[("V", vg)])
        s_r = p.c["S"]
        gate_mm(1, [("V", vg)])
        t_g = p.c["T"]
        p.V(lambda e: e.tensor_add(out=gtmp[:], in0=psS[0:NSL, 0:1024],
                                   in1=gh_bf[:, D:2 * D]),
            waits=[("T", t_g), ("S", s_r)])
        p.V(lambda e: e.tensor_add(out=gtmp[:], in0=gtmp[:], in1=bih_bc[:, D:2 * D]),
            inc=True)
        vg = p.c["V"]
        p.Sc(lambda e: e.activation(out=z_sb[:], in_=gtmp[:], func=ACT.Sigmoid),
             inc=True, waits=[("V", vg)])
        s_z = p.c["S"]
        gate_mm(2, [("V", vg)])
        t_g = p.c["T"]
        p.V(lambda e: e.tensor_mul(out=gtmp2[:], in0=r_sb[:], in1=gh_bf[:, 2 * D:3 * D]),
            waits=[("T", t_g), ("S", s_r)])
        p.V(lambda e: e.tensor_add(out=gtmp2[:], in0=gtmp2[:], in1=psS[0:NSL, 0:1024]))
        p.V(lambda e: e.tensor_add(out=gtmp2[:], in0=gtmp2[:], in1=bih_bc[:, 2 * D:3 * D]),
            inc=True)
        vg = p.c["V"]
        p.Sc(lambda e: e.activation(out=gtmp2[:], in_=gtmp2[:], func=ACT.Tanh),
             inc=True, waits=[("V", vg)])
        s_n = p.c["S"]
        p.V(lambda e: e.tensor_sub(out=gtmp[:], in0=si0b[:], in1=gtmp2[:]),
            waits=[("S", s_n), ("S", s_z)])
        p.V(lambda e: e.tensor_mul(out=gtmp[:], in0=z_sb[:], in1=gtmp[:]))
        p.V(lambda e: e.tensor_add(out=snew_f[:], in0=gtmp2[:], in1=gtmp[:]))
        p.V(lambda e: e.tensor_copy(out=snew_bf[:], in_=snew_f[:]), inc=True)
        v_sn = p.c["V"]
        if dbg and b == 0:
            p.Gdma(taps["t_snew"][:], snew_f[:], waits=[("V", v_sn)])
        for jc in range(NJ):
            p.T(lambda e, jc=jc: e.matmul(out=pv2[:, jc * 64:(jc + 1) * 64], lhsT=snew_bf[:, jc * 128:(jc + 1) * 128], rhs=identb[0:NSL, 0:NSL], start=True, stop=True),
                inc=(jc == NJ - 1), waits=[("V", v_sn)] if jc == 0 else ())
        tsnT = p.c["T"]
        p.V(lambda e: e.tensor_copy(out=snewT[:], in_=pv2[:, 0:NJ * 64]),
            inc=True, waits=[("T", tsnT)])
        v_snT = p.c["V"]

        for dc in range(NJ):
            p.T(lambda e, dc=dc: e.matmul(
                    out=pv[0:NSL, 0:DQ], lhsT=snewT[:, dc * NSL:(dc + 1) * NSL],
                    rhs=wq_sb[:, dc * DQ:(dc + 1) * DQ],
                    start=(dc == 0), stop=(dc == NJ - 1)),
                waits=[("V", v_snT), ("DG", dgwq)] if dc == 0 else ())
        for dc in range(NJ):
            p.T(lambda e, dc=dc: e.matmul(
                    out=pv[0:NSL, 512:512 + DQ], lhsT=snewT[:, dc * NSL:(dc + 1) * NSL],
                    rhs=wk_sb[:, dc * DQ:(dc + 1) * DQ],
                    start=(dc == 0), stop=(dc == NJ - 1)),
                inc=(dc == NJ - 1), waits=[("DG", dgwk)] if dc == 0 else ())
        tqk = p.c["T"]
        p.V(lambda e: e.tensor_add(out=q2_bf[:], in0=pv[0:NSL, 0:DQ], in1=bq_bc[:]),
            waits=[("T", tqk), ("DG", dg_b2)])
        p.V(lambda e: e.tensor_add(out=k2_bf[:], in0=pv[0:NSL, 512:512 + DQ],
                                   in1=bk_bc[:]), inc=True)
        vqk = p.c["V"]
        for c2 in range(2):
            p.T(lambda e, c2=c2: e.matmul(out=pv2[:, c2 * 64:(c2 + 1) * 64], lhsT=q2_bf[:, c2 * 128:(c2 + 1) * 128], rhs=identb[0:NSL, 0:NSL], start=True, stop=True),
                waits=[("V", vqk)] if c2 == 0 else ())
        for c2 in range(2):
            p.T(lambda e, c2=c2: e.matmul(out=pv2[:, 128 + c2 * 64:128 + (c2 + 1) * 64], lhsT=k2_bf[:, c2 * 128:(c2 + 1) * 128], rhs=identb[0:NSL, 0:NSL], start=True, stop=True),
                inc=(c2 == 1))
        tqkT = p.c["T"]
        p.V(lambda e: e.tensor_copy(out=qT2[:], in_=pv2[:, 0:128]), waits=[("T", tqkT)])
        p.V(lambda e: e.tensor_copy(out=kT2[:], in_=pv2[:, 128:256]), inc=True)
        vqkT = p.c["V"]
        for c2 in range(2):
            p.T(lambda e, c2=c2: e.matmul(
                    out=pv[0:NSL, 768:768 + NSL], lhsT=qT2[:, c2 * NSL:(c2 + 1) * NSL],
                    rhs=kT2[:, c2 * NSL:(c2 + 1) * NSL],
                    start=(c2 == 0), stop=(c2 == 1)),
                inc=(c2 == 1), waits=[("V", vqkT)] if c2 == 0 else ())
        tsc2 = p.c["T"]
        p.V(lambda e: e.reduce_max(out=cols[0:NSL, 13:14],
                                   in_=pv[0:NSL, 768:768 + NSL], axis=AXX),
            inc=True, waits=[("T", tsc2)])
        vacc = p.c["V"]
        p.V(lambda e: e.tensor_scalar(out=cols[0:NSL, 14:15], in0=cols[0:NSL, 13:14],
                                      scalar1=-0.0625, scalar2=None, op0=ALU.mult),
            inc=True, waits=[("V", vacc)])
        vng = p.c["V"]
        p.Sc(lambda e: e.activation(out=e2_sb[:], in_=pv[0:NSL, 768:768 + NSL],
                                    func=ACT.Exp, scale=0.0625,
                                    bias=cols[0:NSL, 14:15],
                                    accum_out=cols[0:NSL, 15:16]),
             inc=True, waits=[("V", vng)])
        se2 = p.c["S"]
        p.Sc(lambda e: e.activation(out=cols[0:NSL, 7:8], in_=cols[0:NSL, 15:16],
                                    func=ACT.Ln), inc=True, waits=[("S", se2)])
        sr20 = p.c["S"]
        p.Sc(lambda e: e.activation(out=cols[0:NSL, 6:7], in_=cols[0:NSL, 7:8],
                                    func=ACT.Exp, scale=-1.0), inc=True,
             waits=[("S", sr20)])
        sr2e = p.c["S"]
        p.V(lambda e: e.tensor_scalar(out=aP_bf[:], in0=e2_sb[:],
                                      scalar1=cols[0:NSL, 6:7], scalar2=None,
                                      op0=ALU.mult), inc=True, waits=[("S", sr2e)])
        vaP = p.c["V"]
        p.T(lambda e: e.matmul(out=pv2[0:NSL, 0:NSL], lhsT=aP_bf[:], rhs=identb[0:NSL, 0:NSL], start=True, stop=True), inc=True,
            waits=[("V", vaP)])
        taP = p.c["T"]
        p.V(lambda e: e.tensor_copy(out=aPT[:], in_=pv2[0:NSL, 0:NSL]),
            inc=True, waits=[("T", taP)])
        vaPT = p.c["V"]
        for hf in range(2):
            for dc in range(NJ):
                p.T(lambda e, hf=hf, dc=dc: e.matmul(
                        out=psS[0:NSL, hf * 512:(hf + 1) * 512],
                        lhsT=snewT[:, dc * NSL:(dc + 1) * NSL],
                        rhs=wv_sb[:, dc * D + hf * 512: dc * D + hf * 512 + 512],
                        start=(dc == 0), stop=(dc == NJ - 1)),
                    inc=(hf == 1 and dc == NJ - 1),
                    waits=[("DG", dgwv), ("V", v_sn)] if (hf == 0 and dc == 0) else ())
        tvs = p.c["T"]
        p.V(lambda e: e.tensor_add(out=vs_sb[:], in0=psS[0:NSL, 0:1024], in1=bv_bc[:]),
            inc=True, waits=[("T", tvs)])
        vvs = p.c["V"]
        for hf in range(2):
            p.T(lambda e, hf=hf: e.matmul(
                    out=pv[0:NSL, hf * 512:(hf + 1) * 512], lhsT=aPT[:],
                    rhs=vs_sb[:, hf * 512:(hf + 1) * 512], start=True, stop=True),
                inc=(hf == 1), waits=[("V", vvs), ("V", vaPT)] if hf == 0 else ())
        tcm = p.c["T"]
        p.V(lambda e: e.tensor_add(out=y2_f[:], in0=pv[0:NSL, 0:1024], in1=snew_f[:]),
            waits=[("T", tcm)])
        p.V(lambda e: e.bn_stats(out=bnst[0:NSL, 0:6], in_=y2_f[:, 0:512]))
        p.V(lambda e: e.bn_stats(out=bnst[0:NSL, 6:12], in_=y2_f[:, 512:1024]), inc=True)
        vbs = p.c["V"]
        p.V(lambda e: e.bn_aggr(out=bnagg[0:NSL, :], in_=bnst[0:NSL, :]), inc=True,
            waits=[("V", vbs)])
        vst2 = p.c["V"]
        p.Sc(lambda e: e.activation(out=cols[0:NSL, 5:6], in_=bnagg[0:NSL, 1:2],
                                    func=ACT.Ln, bias=epscol[0:NSL, 0:1], scale=1.0),
             inc=True, waits=[("V", vst2)])
        sl1 = p.c["S"]
        p.Sc(lambda e: e.activation(out=cols[0:NSL, 6:7], in_=cols[0:NSL, 5:6],
                                    func=ACT.Exp, scale=-0.5), inc=True,
             waits=[("S", sl1)])
        sr2 = p.c["S"]
        p.V(lambda e: e.tensor_scalar(
                out=y2_f[:], in0=y2_f[:], scalar1=bnagg[0:NSL, 0:1],
                scalar2=cols[0:NSL, 6:7], op0=ALU.subtract, op1=ALU.mult),
            waits=[("S", sr2)])
        p.V(lambda e: e.tensor_mul(out=y2_f[:], in0=y2_f[:], in1=lng_bc[:]))
        p.V(lambda e: e.tensor_add(out=y2n_bf[:], in0=y2_f[:], in1=lnb_bc[:]),
            inc=True)
        v_y2 = p.c["V"]
        if dbg and b == 0:
            p.Gdma(taps["t_y2n"][:], y2n_bf[:], waits=[("V", v_y2)])
        for jc in range(NJ):
            p.T(lambda e, jc=jc: e.matmul(out=pv2[:, jc * 64:(jc + 1) * 64], lhsT=y2n_bf[:, jc * 128:(jc + 1) * 128], rhs=identb[0:NSL, 0:NSL], start=True, stop=True),
                inc=(jc == NJ - 1), waits=[("V", v_y2)] if jc == 0 else ())
        ty2T = p.c["T"]
        p.V(lambda e: e.tensor_copy(out=y2nT[:], in_=pv2[:, 0:NJ * 64]),
            inc=True, waits=[("T", ty2T)])
        v_y2T = p.c["V"]
        for hf in range(2):
            for dc in range(NJ):
                p.T(lambda e, hf=hf, dc=dc: e.matmul(
                        out=psS[0:NSL, hf * 512:(hf + 1) * 512],
                        lhsT=y2nT[:, dc * NSL:(dc + 1) * NSL],
                        rhs=wvp_sb[:, dc * D + hf * 512: dc * D + hf * 512 + 512],
                        start=(dc == 0), stop=(dc == NJ - 1)),
                    inc=(hf == 1 and dc == NJ - 1),
                    waits=[("DG", dgwvp), ("V", v_y2T)] if (hf == 0 and dc == 0) else ())
        tsv = p.c["T"]
        p.V(lambda e: e.tensor_add(out=sv_bf[:], in0=psS[0:NSL, 0:1024], in1=bvp_bc[:]),
            inc=True, waits=[("T", tsv)])
        v_sv = p.c["V"]
        if dbg and b == 0:
            p.Gdma(taps["t_sv"][:], sv_bf[:], waits=[("V", v_sv)])

        v_grp = p.c["V"]
        s_ot = p.c["S"]
        evB9 = {0: p.c["V"], 1: p.c["V"]}
        for tg in range(4):
            for dc in range(NJ):
                qtr = dc % 4
                w_ = [("V", v_sv), ("V", v_alpha), ("V", v_grp), ("S", s_ot)]
                p.T(lambda e, dc=dc, tg=tg, q_=qtr: e.matmul(
                        out=psS[:, q_ * 512:(q_ + 1) * 512],
                        lhsT=sv_bf[0:NSL, dc * 128:(dc + 1) * 128],
                        rhs=alphaT[0:NSL, tg * 512:(tg + 1) * 512],
                        start=True, stop=True),
                    inc=(qtr == 3 or dc == NJ - 1),
                    waits=w_ if dc == 0 else ([("S", s_ot)] if dc == 4 else ()))
                if qtr == 3 or dc == NJ - 1:
                    tmm = p.c["T"]
                    base = (dc // 4) * 4
                    nq = dc - base + 1
                    p.Sc(lambda e, base=base, nq=nq: e.copy(
                            out=otbuf[:, base * 512:(base + nq) * 512],
                            in_=psS[:, 0:nq * 512] if base == 0 else
                                psS[:, 0:nq * 512]),
                         inc=True, waits=[("T", tmm)])
                    s_ot = p.c["S"]
            for u in range(4):
                tt = tg * 4 + u
                sl = tt % 2
                dy = p.Ydma(ystage[:, sl * D:(sl + 1) * D],
                            y_d[b * S + tt * 128: b * S + (tt + 1) * 128, :],
                            waits=[("V", ystage_cons[sl])])
                w_ = [("S", s_ot), ("DG", dgwo), ("V", evB9[sl])]
                for hf in range(2):
                    for dc in range(NJ):
                        p.T(lambda e, u=u, hf=hf, dc=dc, sl=sl: e.matmul(
                                out=psS[:, sl * 1024 + hf * 512:
                                        sl * 1024 + hf * 512 + 512],
                                lhsT=otbuf[:, dc * 512 + u * 128: dc * 512 + u * 128 + 128],
                                rhs=wo_sb[:, dc * D + hf * 512: dc * D + hf * 512 + 512],
                                start=(dc == 0), stop=(dc == NJ - 1)),
                            inc=(hf == 1 and dc == NJ - 1),
                            waits=w_ if (hf == 0 and dc == 0) else ())
                tmm = p.c["T"]
                p.V(lambda e, sl=sl: e.scalar_tensor_tensor(
                        out=fo_sb[:], in0=psS[:, sl * 1024:(sl + 1) * 1024], scalar=1.0,
                        in1=bo_bc[:], op0=ALU.bypass, op1=ALU.add),
                    inc=True, waits=[("T", tmm), ("DG", dg_b2)])
                vfo = p.c["V"]
                evB9[sl] = vfo
                p.V(lambda e: e.bn_stats(out=bnst[:, 0:6], in_=fo_sb[:, 0:512]),
                    waits=[("V", vfo)])
                p.V(lambda e: e.bn_stats(out=bnst[:, 6:12], in_=fo_sb[:, 512:1024]),
                    inc=True)
                vbs = p.c["V"]
                p.V(lambda e: e.bn_aggr(out=bnagg[:], in_=bnst[:]), inc=True,
                    waits=[("V", vbs)])
                vst3 = p.c["V"]
                p.Sc(lambda e: e.activation(out=cols[:, 5:6], in_=bnagg[:, 1:2],
                                            func=ACT.Ln, bias=epscol[:, 0:1], scale=1.0),
                     inc=True, waits=[("V", vst3)])
                sl1 = p.c["S"]
                p.Sc(lambda e: e.activation(out=cols[:, 6:7], in_=cols[:, 5:6],
                                            func=ACT.Exp, scale=-0.5), inc=True,
                     waits=[("S", sl1)])
                sr3 = p.c["S"]
                p.V(lambda e: e.tensor_scalar(
                        out=fo_sb[:], in0=fo_sb[:], scalar1=bnagg[:, 0:1],
                        scalar2=cols[:, 6:7], op0=ALU.subtract, op1=ALU.mult),
                    waits=[("S", sr3), ("DY", dy)])
                p.V(lambda e, sl=sl: e.tensor_add(
                        out=ystage[:, sl * D:(sl + 1) * D], in0=fo_sb[:],
                        in1=ystage[:, sl * D:(sl + 1) * D]), inc=True)
                vout = p.c["V"]
                p.Ydma(out_d[b * S + tt * 128: b * S + (tt + 1) * 128, :],
                       ystage[:, sl * D:(sl + 1) * D], waits=[("V", vout)])
                ystage_cons[sl] = vout
            v_grp = p.c["V"]

    fctx.close()
    p.Ywait([("DY", p.c["DY"])])

    with nc.Block() as block:
        @block.tensor
        def _(e):
            for fn in p.q["T"]:
                fn(e, sems)

        @block.vector
        def _(e):
            for fn in p.q["V"]:
                fn(e, sems)

        @block.scalar
        def _(e):
            for fn in p.q["S"]:
                fn(e, sems)

        @block.gpsimd
        def _(e):
            for fn in p.q["G"]:
                fn(e, sems)

        @block.sync
        def _(e):
            for fn in p.q["Y"]:
                fn(e, sems)

    ctx.close()
    return nc, taps


def x_d_slice(dt, b, tt):
    return dt["x"][b * S + tt * 128: b * S + (tt + 1) * 128, :]


def _prep_core_inputs(inputs, core):
    A = lambda v: np.ascontiguousarray(np.asarray(v), dtype=np.float32)
    x = np.asarray(inputs["L0_attn_x"])
    cosb, sinb = _rope_tables()
    return {
        "x": A(x[core * BL:(core + 1) * BL].reshape(BL * S, D)),
        "WqT": A(np.asarray(inputs["L0_attn_Wq"]).T),
        "WkT": A(np.asarray(inputs["L0_attn_Wk"]).T),
        "WvT": A(np.asarray(inputs["L0_attn_Wv"]).T),
        "WoT": A(np.asarray(inputs["L0_attn_Wo"]).T),
        "w1": A(inputs["L0_ffn_w1"]),
        "w2": A(inputs["L0_ffn_w2"]),
        "WihT": A(np.asarray(inputs["L0_ffn_Wih"]).T),
        "WhhT": A(np.asarray(inputs["L0_ffn_Whh"]).T),
        "wq": A(inputs["L0_ffn_wq"]),
        "wk": A(inputs["L0_ffn_wk"]),
        "wv": A(inputs["L0_ffn_wv"]),
        "wvp": A(inputs["L0_ffn_wvp"]),
        "wo": A(inputs["L0_ffn_wo"]),
        "slot_init": A(inputs["L0_ffn_slot_init"]),
        "b1r": A(np.asarray(inputs["L0_ffn_b1"])[None]),
        "b2bc": A(np.tile(np.asarray(inputs["L0_ffn_b2"])[None], (128, 1))),
        "taur": A(np.tile(np.asarray(inputs["L0_ffn_tau"]).reshape(1, 1), (128, 1))),
        "bihc": A(np.tile(np.asarray(inputs["L0_ffn_bih"])[None], (64, 1))),
        "bhhc": A(np.tile(np.asarray(inputs["L0_ffn_bhh"])[None], (64, 1))),
        "bqc": A(np.tile(np.asarray(inputs["L0_ffn_bq"])[None], (64, 1))),
        "bkc": A(np.tile(np.asarray(inputs["L0_ffn_bk"])[None], (64, 1))),
        "bvc": A(np.tile(np.asarray(inputs["L0_ffn_bv"])[None], (64, 1))),
        "lngc": A(np.tile(np.asarray(inputs["L0_ffn_ln_g"])[None], (64, 1))),
        "lnbc": A(np.tile(np.asarray(inputs["L0_ffn_ln_b"])[None], (64, 1))),
        "bvpc": A(np.tile(np.asarray(inputs["L0_ffn_bvp"])[None], (64, 1))),
        "boc": A(np.tile(np.asarray(inputs["L0_ffn_bo"])[None], (128, 1))),
        "identc": np.eye(128, dtype=np.float32),
        "cosc": cosb,
        "sinc": sinb,
        "mextc": _mext_table(),
    }


def kernel(**inputs):
    nc, _ = build_nc(dbg=False)
    in_maps = [_prep_core_inputs(inputs, c) for c in range(8)]
    res = run_bass_kernel_spmd(nc, in_maps, core_ids=list(range(8)))
    return np.concatenate(
        [np.asarray(r["out"], np.float32).reshape(BL, S, D) for r in res.results], axis=0)
